# version 37
# baseline (speedup 1.0000x reference)
import numpy as np
import ml_dtypes
import concourse.bass as bass
import concourse.mybir as mybir
from concourse.alu_op_type import AluOpType
from concourse.bass_core_rust import BassInstruction as _BI


def _seminc(self, sem, k):
    sem.inc(self, k)
    return self


_BI._SEMINC = _seminc


F32 = mybir.dt.float32
BF16 = mybir.dt.bfloat16
AF = mybir.ActivationFunctionType
AX = mybir.AxisListType
bfdt = ml_dtypes.bfloat16

FULL_CFG = dict(S=2048, D=2048, NH=32, HGATE=8192, TSA=512, TSF=512, debug=False)
MINI_CFG = dict(S=256, D=256, NH=4, HGATE=1024, TSA=128, TSF=256, debug=True)


class SemPool:

    def __init__(self, nc, size=90, sw=14):
        self.handles = [nc.alloc_semaphore(f"s{i}") for i in range(size)]
        self.vals = [0] * size
        self.next = 0
        self.lim = size - sw
        self.swnext = self.lim

    def take(self, kind):
        if kind == "sw":
            i = self.swnext
            self.swnext = self.lim + (self.swnext + 1 - self.lim) % \
                (len(self.handles) - self.lim)
            return i
        i = self.next
        self.next = (self.next + 1) % self.lim
        return i


_POOL = [None]


class Sem:
    def __init__(self, nc, name, kind="hw"):
        pool = _POOL[0]
        self.pool = pool
        self.i = pool.take(kind)
        self.h = pool.handles[self.i]
        self.base = pool.vals[self.i]
        self.n = 0

    def inc(self, ins, k=1):
        ins.then_inc(self.h, k)
        self.n += k
        self.pool.vals[self.i] += k
        return self.n

    def wait(self, eng, v):
        eng.wait_ge(self.h, self.base + v)


def build(cfg):
    S, D, NH, HGATE = cfg["S"], cfg["D"], cfg["NH"], cfg["HGATE"]
    TSA, TSF = cfg["TSA"], cfg["TSF"]
    DBG = cfg.get("debug", False)
    HD = 64
    assert D == NH * HD
    NCD = D // 128
    NT = S // 128
    NSA = S // TSA
    NSF = S // TSF
    NTQA = TSA // 128
    NTSF = TSF // 128
    HGU2 = 2 * HGATE
    NHC = HGATE // 128
    WHALF = D // 2
    NFS_H = WHALF // 128
    TTW = min(512, S)
    NTT = S // TTW
    DTW = min(512, WHALF)
    NDT = WHALF // DTW
    GUB = min(1024, HGU2)
    NGUB = HGU2 // GUB
    NPAIR = GUB // 256
    VTC = 65 * NH
    NDS = D // 128
    DSOCT = min(8, NDS)
    NP4 = NDS // DSOCT
    assert TSF <= 512
    clamp_half = float(np.float32(bfdt(1e-8)) * np.float32(0.5))

    nc = bass.Bass("TRN2", target_bir_lowering=False, debug=False)
    _POOL[0] = SemPool(nc)

    x_d = nc.dram_tensor("x", [S, D], F32, kind="ExternalInput")
    w_d = {nm: nc.dram_tensor(nm, [D, D], F32, kind="ExternalInput")
           for nm in ("wq", "wk", "wv", "wo")}
    b_d = {nm: nc.dram_tensor(nm, [1, D], F32, kind="ExternalInput")
           for nm in ("bq", "bk", "bv", "bo")}
    wgu_d = nc.dram_tensor("wgu", [HGU2, D], F32, kind="ExternalInput")
    wp_d = nc.dram_tensor("wp", [D, HGATE], F32, kind="ExternalInput")
    out_d = nc.dram_tensor("out", [S, D], F32, kind="ExternalOutput")

    wb_d = {nm: nc.dram_tensor(nm + "b", [D, D], BF16)
            for nm in ("wq", "wk", "wv", "wo")}
    wgub_d = nc.dram_tensor("wgub", [HGU2, D], BF16)
    wpb_d = nc.dram_tensor("wpb", [D, HGATE], BF16)
    q_sc = nc.dram_tensor("q_sc", [D, S], BF16)
    k_sc = nc.dram_tensor("k_sc", [D, S], BF16)
    o_sc = nc.dram_tensor("o_sc", [S, D], BF16)
    r2_sc = nc.dram_tensor("r2_sc", [S, D], F32)

    SB_BASE = 18560
    cur = [SB_BASE]

    OFF = {}

    def sb(name, shape, dtype, at=None):
        if at is None:
            at = cur[0]
        OFF[name] = at
        t = nc.alloc_sbuf_tensor_at(name, shape, dtype, offset=at)
        esz = 2 if dtype == BF16 else 4
        end = at + int(np.prod(shape[1:])) * esz
        cur[0] = max(cur[0], (end + 63) // 64 * 64)
        return t

    consts = sb("consts", [128, 8], F32)
    iota_t = sb("iota_t", [128, 128], F32)
    ident = sb("ident", [128, 128], BF16)
    ident32 = sb("ident32", [128, 128], F32)
    tri = sb("tri", [128, 128], BF16)
    trineg = sb("trineg", [128, 128], BF16)
    ones32 = sb("ones32", [128, 128], F32)
    onesbf = sb("onesbf", [128, 16], BF16)
    bqt = sb("bqt", [128, NCD], F32)
    bkt = sb("bkt", [128, NCD], F32)
    bvt = sb("bvt", [128, NCD], F32)
    stats = sb("stats", [128, 128], F32)
    rs_t = sb("rs_t", [128, 8], F32)
    rs_all = sb("rs_all", [128, 2 * 4 * 32], F32)
    invt = sb("invt", [128, 8], F32)
    gsumb = [sb(f"gsumb{i}", [128, 128], F32) for i in range(2)]
    shalfb = [sb(f"shalfb{i}", [128, 128], BF16) for i in range(2)]
    bo_nat = sb("bo_nat", [1, D], F32)
    ssb = sb("ssb", [1, TSF], F32)
    qbuf = sb("qbuf", [128, 2 * TSA], BF16)
    sg_st = sb("sg_st", [128, 512], F32)
    sq_st = sb("sq_st", [128, 512], BF16)
    ev1_st = sb("ev1_st", [128, 512], F32)
    r3_st = sb("r3_st", [128, 1024], F32)
    xn_st = sb("xn_st", [128, D], BF16)
    vstg = sb("vstg", [128, 512], BF16)
    pkstg = [sb(f"pkstg{i}", [128, 512], BF16) for i in range(2)]
    kbuf = sb("kbuf", [128, 2 * 2048], BF16, OFF["ev1_st"])

    R1 = (cur[0] + 63) // 64 * 64
    R1_SZ = max(2 * NCD * S, 36 * D)
    R2 = R1 + R1_SZ
    R2_SZ = max(2 * NT * VTC, 4 * NCD * GUB,
                2 * NCD * WHALF + 8 * WHALF, 12 * D)
    R3 = R2 + R2_SZ
    R3_SZ = max(2 * NCD * WHALF,
                2 * NT * TSA + 2 * NTQA * D,
                4 * WHALF,
                2 * NCD * TSF + 16 * DSOCT * 128)
    assert R3 + R3_SZ <= 229376, (R1, R2, R3, R3 + R3_SZ)

    xnT = sb("xnT", [128, NCD * S], BF16, R1)
    OT = sb("OT", [128, NCD * S], BF16, R1)
    hh = sb("hh", [128, NHC * TSF], BF16, R1)
    wstrip = [sb(f"wstrip{i}", [128, 4 * D], BF16, R1 + 8 * D * i) for i in range(2)]
    qA = sb("qA", [128, 4 * D], BF16, R1 + 16 * D)
    qB = sb("qB", [128, 4 * D], BF16, R1 + 24 * D)
    cstb = [sb(f"cstb{i}", [128, D], BF16, R1 + 32 * D + 2 * D * i) for i in range(2)]

    VT = sb("VT", [128, NT * VTC], BF16, R2)
    woT_h = sb("woT_h", [128, NCD * WHALF], BF16, R2)
    xcb = [sb(f"xcb{i}", [128, WHALF], F32, R2 + 2 * NCD * WHALF + 4 * WHALF * i)
           for i in range(2)]
    gubuf = [sb(f"gubuf{i}", [128, NCD * GUB], BF16, R2 + 2 * NCD * GUB * i)
             for i in range(2)]
    xc1 = [sb(f"xc1_{i}", [128, D], F32, R2 + 4 * D * i) for i in range(2)]
    sqsc = sb("sqsc", [128, D], F32, R2 + 8 * D)

    wT_h = sb("wT_h", [128, NCD * WHALF], BF16, R3)
    PT = sb("PT", [128, NT * TSA], BF16, R3)
    O_stg = sb("O_stg", [128, NTQA * D], BF16, R3 + 2 * NT * TSA)
    r2stg = sb("r2stg", [128, WHALF], F32, R3)
    xn2T = sb("xn2T", [128, NCD * TSF], BF16, R3)
    wpbuf = [sb(f"wpbuf{i}", [128, DSOCT * 128], BF16,
                R3 + 2 * NCD * TSF + 2 * DSOCT * 128 * i) for i in range(4)]
    xn_stB = [sb(f"xn_stB{i}", [128, D], BF16,
                 R3 + 2 * NCD * TSF + 2 * D * i) for i in range(2)]
    rjb = [sb(f"rjb{i}", [128, DSOCT * 128], F32,
              R3 + 2 * NCD * TSF + 8 * DSOCT * 128 + 4 * DSOCT * 128 * i)
           for i in range(2)]

    PS_A = nc.alloc_psum_tensor("PS_A", [128, 2048], F32)
    PS_C = nc.alloc_psum_tensor("PS_C", [128, 2048], F32)
    PS_TR = PS_C.ap().bitcast(BF16)[:, 3072:4096]

    def p4bank(dsl):
        if dsl < 4:
            return PS_A.ap()[:, 512 * dsl:512 * dsl + TSF]
        return PS_C.ap()[:, 512 * (dsl - 4):512 * (dsl - 4) + TSF]

    nc.const_aps.register(F32, 0.125, consts.ap()[:, 0:1])
    nc.const_aps.register(F32, -2.0, consts.ap()[:, 1:2])
    nc.const_aps.register(F32, 1e-5, consts.ap()[:, 2:3])
    nc.const_aps.register(F32, 1.1920929e-07, consts.ap()[:, 3:4])
    nc.const_aps.register(F32, 1.0, consts.ap()[:, 5:6])

    dbg_d = {}
    if DBG:
        for nm, sh, dt in [("xnT", [128, NCD * S], BF16),
                           ("VT", [128, NT * VTC], BF16),
                           ("hh", [128, NHC * TSF], BF16),
                           ("xn2T", [128, NCD * TSF], BF16),
                           ("Ostg", [128, NTQA * D], BF16)]:
            dbg_d[nm] = nc.dram_tensor("dbg_" + nm, sh, dt)

    def dump(name, src_ap):
        if name not in dbg_d:
            return
        with nc.Block() as b:
            sem = Sem(nc, f"dbg{name}")

            @b.sync
            def _(sy):
                sy.dma_start(dbg_d[name].ap(), src_ap)._SEMINC(sem, 16)
                sem.wait(sy, 16)

    with nc.Block() as blk:
        @blk.vector
        def _(v):
            for i, val in enumerate([0.125, -2.0, 1e-5, 1.1920929e-07, 0.0, 1.0]):
                v.memset(consts[:, i:i + 1], val)
            v.memset(ones32[:], 1.0)
            v.memset(onesbf[:], 1.0)

        @blk.gpsimd
        def _(g):
            g.iota(iota_t[:], [[1, 128]], channel_multiplier=-1,
                   allow_small_or_imprecise_dtypes=True)

    with nc.Block() as blk:
        @blk.vector
        def _(v):
            v.tensor_scalar(ident[:], iota_t[:], 0.0, None, op0=AluOpType.is_equal)
            v.tensor_scalar(ident32[:], iota_t[:], 0.0, None, op0=AluOpType.is_equal)
            v.tensor_scalar(tri[:], iota_t[:], 0.0, None, op0=AluOpType.is_ge)
            v.tensor_scalar(trineg[:], iota_t[:], 0.0, -1e9,
                            op0=AluOpType.is_lt, op1=AluOpType.mult)

    with nc.Block() as blk:
        ld = Sem(nc, "p0a_ld")
        mm = Sem(nc, "p0a_mm")
        ev = Sem(nc, "p0a_ev")
        dests = [("bq", bqt), ("bk", bkt), ("bv", bvt), ("bo", None)]

        @blk.sync
        def _(sy):
            for i, (nm, dst) in enumerate(dests):
                if i > 0:
                    ev.wait(sy, i)
                sy.dma_start(bo_nat[:], b_d[nm].ap())._SEMINC(ld, 16)

        @blk.tensor
        def _(t):
            for i, (nm, dst) in enumerate(dests):
                ld.wait(t, 16 * (i + 1))
                if dst is None:
                    t.nop()._SEMINC(mm, 1)
                    continue
                for j in range(NCD):
                    ins = t.matmul(PS_C.ap()[:, 512 + j:512 + j + 1],
                                   bo_nat.ap()[0:1, 128 * j:128 * (j + 1)],
                                   ones32.ap()[0:1, 0:1], start=True, stop=True)
                ins._SEMINC(mm, 1)

        @blk.vector
        def _(v):
            for i, (nm, dst) in enumerate(dests):
                mm.wait(v, i + 1)
                if dst is None:
                    v.engine_nop()._SEMINC(ev, 1)
                else:
                    v.tensor_copy(dst[:, 0:NCD], PS_C.ap()[:, 512:512 + NCD])._SEMINC(ev, 1)

    def wcast_sems(tag):
        return dict(ld=[Sem(nc, f"{tag}l{i}", kind="sw") for i in range(2)],
                    st=[Sem(nc, f"{tag}s{i}") for i in range(2)])

    def wcast_gp(g, nm, q):
        for c in range(NCD):
            b = c % 2
            if c >= 2:
                q["st"][b].wait(g, 16 * (c // 2))
            g.dma_start(cstb[b][:], w_d[nm].ap()[128 * c:128 * (c + 1), :]) \
                ._SEMINC(q["ld"][b], 16)

    def wcast_store_one(s, nm, q, c):
        b = c % 2
        q["ld"][b].wait(s, 16 * (c // 2 + 1))
        s.dma_start(wb_d[nm].ap()[128 * c:128 * (c + 1), :], cstb[b][:]) \
            ._SEMINC(q["st"][b], 16)

    def wcast_scalar(s, nm, q):
        for c in range(NCD):
            wcast_store_one(s, nm, q, c)
        q["st"][0].wait(s, q["st"][0].n)
        q["st"][1].wait(s, q["st"][1].n)

    strips = [("gu", si) for si in range(HGU2 // 512)] + \
             [("p", si) for si in range(D // 128)]

    def sfree(kind):
        return 4 * D if kind == "gu" else HGATE

    def shbcast(b, nf):
        return shalfb[b].ap()[:, 0:nf // 64].unsqueeze(2) \
            .broadcast_to([128, nf // 64, 64])

    st_counts = []
    _n = 0
    for kind, si in strips:
        _n += 4 if kind == "gu" else 1
        st_counts.append(_n)

    def _p0c_store(s, strip, st):
        kind, si = strip
        nf = sfree(kind)
        if kind == "gu":
            for c in range(4):
                gsub = si * 4 + c
                if 512 * si + 128 * c < HGATE:
                    row = 256 * gsub
                else:
                    row = 256 * (gsub - HGATE // 128) + 128
                s.dma_start(wgub_d.ap()[row:row + 128, :],
                            qA[:, D * c:D * (c + 1)])._SEMINC(st, 16)
        else:
            s.dma_start(wpb_d.ap()[128 * si:128 * (si + 1), :],
                        qA[:, 0:nf])._SEMINC(st, 16)

    def quant_sems(tag):
        return dict(ld=[Sem(nc, f"{tag}l{i}", kind="sw") for i in range(2)],
                    sml=Sem(nc, f"{tag}m"),
                    ads=Sem(nc, f"{tag}a"), st=Sem(nc, f"{tag}s"))

    def quant_gp(g, lo, hi, q):
        ld, ads = q["ld"], q["ads"]
        for i in range(lo, hi):
            kind, si = strips[i]
            j = i - lo
            b = j % 2
            nf = sfree(kind)
            if j >= 2:
                ads.wait(g, j - 1)
            if kind == "gu":
                src = wgu_d.ap()[512 * si:512 * (si + 1), :].rearrange(
                    "(c p) d -> p c d", p=128)
                dst = wstrip[b].ap()[:, 0:nf].rearrange("p (c d) -> p c d", d=D)
            else:
                src = wp_d.ap()[128 * si:128 * (si + 1), :]
                dst = wstrip[b][:, 0:nf]
            g.dma_start(dst, src)._SEMINC(ld[b], 16)

    def quant_strip_scalar(s, i, j, q):
        ads, st = q["ads"], q["st"]
        if j > 0:
            ads.wait(s, j)
            _p0c_store(s, strips[i - 1], st)

    def quant_flush_scalar(s, lo, hi, q):
        if hi > lo:
            q["ads"].wait(s, hi - lo)
            _p0c_store(s, strips[hi - 1], q["st"])
            q["st"].wait(s, q["st"].n)

    def quant_strip_vector(v, i, j, q):
        kind, si = strips[i]
        b = j % 2
        nf = sfree(kind)
        ld, sml, ads, st = q["ld"], q["sml"], q["ads"], q["st"]
        ld[b].wait(v, 16 * (j // 2 + 1))
        wgv0 = wstrip[b].ap()[:, 0:nf].rearrange("p (g e) -> p g e", e=64)
        v.tensor_reduce(gsumb[b].ap()[:, 0:nf // 64], wgv0, axis=AX.X,
                        op=AluOpType.add, apply_absolute_value=True)
        v.drain()
        v.tensor_scalar(shalfb[b][:, 0:nf // 64], gsumb[b][:, 0:nf // 64],
                        float(1.0 / 128.0), clamp_half,
                        op0=AluOpType.mult, op1=AluOpType.max)._SEMINC(sml, 1)
        if j >= 1:
            st.wait(v, 16 * st_local(i - j, j))
        v.tensor_scalar(qA.ap()[:, 0:nf].bitcast(mybir.dt.uint16),
                        wstrip[b].ap()[:, 0:nf].bitcast(mybir.dt.uint16),
                        0x7FFF, None, op0=AluOpType.bitwise_and)
        v.drain()
        v.tensor_tensor(qA.ap()[:, 0:nf].rearrange("p (g e) -> p g e", e=64),
                        qA.ap()[:, 0:nf].rearrange("p (g e) -> p g e", e=64),
                        shbcast(b, nf), op=AluOpType.is_gt)
        v.drain()
        v.scalar_tensor_tensor(
            qA.ap()[:, 0:nf].rearrange("p (g e) -> p g e", e=64),
            qA.ap()[:, 0:nf].rearrange("p (g e) -> p g e", e=64), 2.0,
            shbcast(b, nf),
            op0=AluOpType.mult, op1=AluOpType.mult)
        v.tensor_scalar(qB.ap()[:, 0:nf].bitcast(mybir.dt.uint16),
                        wstrip[b].ap()[:, 0:nf].bitcast(mybir.dt.uint16),
                        0x8000, None, op0=AluOpType.bitwise_and)
        v.drain()
        v.tensor_tensor(qA.ap()[:, 0:nf].bitcast(mybir.dt.uint16),
                        qA.ap()[:, 0:nf].bitcast(mybir.dt.uint16),
                        qB.ap()[:, 0:nf].bitcast(mybir.dt.uint16),
                        op=AluOpType.bitwise_or)._SEMINC(ads, 1)

    NSTR = len(strips)
    qsl = []
    per = (NSTR + NSA - 1) // NSA
    for stg in range(NSA):
        qsl.append((min(stg * per, NSTR), min((stg + 1) * per, NSTR)))

    def st_local(lo, j):
        n = 0
        for t in range(lo, lo + j):
            n += 4 if strips[t][0] == "gu" else 1
        return n

    def quant_sched(lo, hi):
        n = hi - lo
        return {int((j + 1) * NH / (n + 1)): j for j in range(n)} if n else {}

    def emit_ln(src_dram, row0, nchunks, dest_T, dest_w, tag,
                aux_gp=None, aux_sc=None, aux_sy=None):
        NEVG = (NCD + 3) // 4

        with nc.Block() as b:
            ld = [Sem(nc, f"{tag}l{i}") for i in range(2)]
            sa = Sem(nc, f"{tag}s")
            dv = Sem(nc, f"{tag}d")
            tr = Sem(nc, f"{tag}t")
            ev = Sem(nc, f"{tag}e")

            @b.sync
            def _(sy):
                for i in range(2 * nchunks):
                    bb = i % 2
                    if i >= 2:
                        j = i - 2
                        if j < nchunks:
                            sa.wait(sy, j + 1)
                            dv.wait(sy, j + 1)
                        else:
                            dv.wait(sy, nchunks + 4 + (j - nchunks) + 1)
                    c = i % nchunks
                    sy.dma_start(xc1[bb][:],
                                 src_dram.ap()[row0 + 128 * c:row0 + 128 * (c + 1), :]) \
                        ._SEMINC(ld[bb], 16)
                if aux_sy is not None:
                    aux_sy(sy)

            @b.gpsimd
            def _(g):
                if aux_gp is not None:
                    aux_gp(g)

            @b.scalar
            def _(s):
                for c in range(nchunks):
                    bb = c % 2
                    ld[bb].wait(s, 16 * (c // 2 + 1))
                    s.activation(sqsc[:], xc1[bb][:], AF.Square,
                                 accum_out=stats[:, 16 + c:17 + c])._SEMINC(sa, 1)
                    s.drain()
                if aux_sc is not None:
                    aux_sc(s)
                dv.wait(s, nchunks + 3)
                s.activation(stats[:, 80:80 + nchunks], stats[:, 64:64 + nchunks],
                             AF.Sqrt, bias=1e-5)._SEMINC(sa, 1)

            @b.vector
            def _(v):
                for c in range(nchunks):
                    bb = c % 2
                    ld[bb].wait(v, 16 * (c // 2 + 1))
                    v.reduce_sum(stats[:, c:c + 1], xc1[bb][:], axis=AX.X) \
                        ._SEMINC(dv, 1)
                sa.wait(v, nchunks)
                v.drain()
                v.tensor_scalar(stats[:, 32:32 + nchunks], stats[:, 0:nchunks],
                                float(1.0 / D), None, op0=AluOpType.mult) \
                    ._SEMINC(dv, 1)
                v.drain()
                v.tensor_mul(stats[:, 48:48 + nchunks], stats[:, 32:32 + nchunks],
                             stats[:, 32:32 + nchunks])._SEMINC(dv, 1)
                v.drain()
                v.scalar_tensor_tensor(stats[:, 64:64 + nchunks],
                                       stats[:, 16:16 + nchunks], float(1.0 / D),
                                       stats[:, 48:48 + nchunks],
                                       op0=AluOpType.mult, op1=AluOpType.subtract) \
                    ._SEMINC(dv, 1)
                sa.wait(v, nchunks + 1)
                v.reciprocal(stats[:, 96:96 + nchunks], stats[:, 80:80 + nchunks]) \
                    ._SEMINC(dv, 1)
                v.drain()
                def ln_apply(c):
                    i = nchunks + c
                    bb = i % 2
                    ld[bb].wait(v, 16 * (i // 2 + 1))
                    if c >= 2:
                        tr.wait(v, NEVG * (c - 1))
                    v.tensor_scalar(xn_stB[c % 2][:], xc1[bb][:],
                                    stats.ap()[:, 32 + c:33 + c],
                                    stats.ap()[:, 96 + c:97 + c],
                                    op0=AluOpType.subtract, op1=AluOpType.mult) \
                        ._SEMINC(dv, 1)

                ln_apply(0)
                for c in range(nchunks):
                    if c + 1 < nchunks:
                        ln_apply(c + 1)
                    for gq in range(NEVG):
                        gsz = min(4, NCD - 4 * gq)
                        tr.wait(v, NEVG * c + gq + 1)
                        pv = bass.AP(PS_TR.tensor, PS_TR.offset,
                                     [[PS_TR.ap[0][0], 128], [128, gsz], [1, 128]])
                        dst = bass.AP(dest_T.ap().tensor,
                                      dest_w * (4 * gq) + 128 * c,
                                      [[dest_T.shape[1], 128], [dest_w, gsz],
                                       [1, 128]])
                        v.tensor_copy(dst, pv)._SEMINC(ev, 1)

            @b.tensor
            def _(t):
                for c in range(nchunks):
                    dv.wait(t, nchunks + 4 + c + 1)
                    for gq in range(NEVG):
                        gsz = min(4, NCD - 4 * gq)
                        if c > 0 or gq > 0:
                            ev.wait(t, NEVG * c + gq)
                        for kk in range(gsz):
                            dc = 4 * gq + kk
                            ins = t.transpose(
                                PS_TR[:, 128 * kk:128 * (kk + 1)],
                                xn_stB[c % 2].ap()[:, 128 * dc:128 * (dc + 1)],
                                ident[:])
                        ins._SEMINC(tr, 1)

    def emit_ln(src_dram, row0, nchunks, dest_T, dest_w, tag,
                aux_gp=None, aux_sc=None, aux_sy=None):
        NEVG = (NCD + 3) // 4

        with nc.Block() as b:
            ld = [Sem(nc, f"{tag}l{i}") for i in range(2)]
            sa = Sem(nc, f"{tag}s")
            dv = Sem(nc, f"{tag}d")
            tr = Sem(nc, f"{tag}t")
            ev = Sem(nc, f"{tag}e")

            @b.sync
            def _(sy):
                for i in range(2 * nchunks):
                    bb = i % 2
                    if i >= 2:
                        j = i - 2
                        if j < nchunks:
                            sa.wait(sy, j + 1)
                            dv.wait(sy, j + 1)
                        else:
                            dv.wait(sy, nchunks + 4 + (j - nchunks) + 1)
                    c = i % nchunks
                    sy.dma_start(xc1[bb][:],
                                 src_dram.ap()[row0 + 128 * c:row0 + 128 * (c + 1), :]) \
                        ._SEMINC(ld[bb], 16)
                if aux_sy is not None:
                    aux_sy(sy)

            @b.gpsimd
            def _(g):
                if aux_gp is not None:
                    aux_gp(g)

            @b.scalar
            def _(s):
                for c in range(nchunks):
                    bb = c % 2
                    ld[bb].wait(s, 16 * (c // 2 + 1))
                    s.activation(sqsc[:], xc1[bb][:], AF.Square,
                                 accum_out=stats[:, 16 + c:17 + c])._SEMINC(sa, 1)
                    s.drain()
                if aux_sc is not None:
                    aux_sc(s)
                dv.wait(s, nchunks + 3)
                s.activation(stats[:, 80:80 + nchunks], stats[:, 64:64 + nchunks],
                             AF.Sqrt, bias=1e-5)._SEMINC(sa, 1)

            @b.vector
            def _(v):
                for c in range(nchunks):
                    bb = c % 2
                    ld[bb].wait(v, 16 * (c // 2 + 1))
                    v.reduce_sum(stats[:, c:c + 1], xc1[bb][:], axis=AX.X) \
                        ._SEMINC(dv, 1)
                sa.wait(v, nchunks)
                v.drain()
                v.tensor_scalar(stats[:, 32:32 + nchunks], stats[:, 0:nchunks],
                                float(1.0 / D), None, op0=AluOpType.mult) \
                    ._SEMINC(dv, 1)
                v.drain()
                v.tensor_mul(stats[:, 48:48 + nchunks], stats[:, 32:32 + nchunks],
                             stats[:, 32:32 + nchunks])._SEMINC(dv, 1)
                v.drain()
                v.scalar_tensor_tensor(stats[:, 64:64 + nchunks],
                                       stats[:, 16:16 + nchunks], float(1.0 / D),
                                       stats[:, 48:48 + nchunks],
                                       op0=AluOpType.mult, op1=AluOpType.subtract) \
                    ._SEMINC(dv, 1)
                sa.wait(v, nchunks + 1)
                v.reciprocal(stats[:, 96:96 + nchunks], stats[:, 80:80 + nchunks]) \
                    ._SEMINC(dv, 1)
                v.drain()
                def ln_apply(c):
                    i = nchunks + c
                    bb = i % 2
                    ld[bb].wait(v, 16 * (i // 2 + 1))
                    if c >= 2:
                        tr.wait(v, NEVG * (c - 1))
                    v.tensor_scalar(xn_stB[c % 2][:], xc1[bb][:],
                                    stats.ap()[:, 32 + c:33 + c],
                                    stats.ap()[:, 96 + c:97 + c],
                                    op0=AluOpType.subtract, op1=AluOpType.mult) \
                        ._SEMINC(dv, 1)

                ln_apply(0)
                for c in range(nchunks):
                    if c + 1 < nchunks:
                        ln_apply(c + 1)
                    for gq in range(NEVG):
                        gsz = min(4, NCD - 4 * gq)
                        tr.wait(v, NEVG * c + gq + 1)
                        pv = bass.AP(PS_TR.tensor, PS_TR.offset,
                                     [[PS_TR.ap[0][0], 128], [128, gsz], [1, 128]])
                        dst = bass.AP(dest_T.ap().tensor,
                                      dest_w * (4 * gq) + 128 * c,
                                      [[dest_T.shape[1], 128], [dest_w, gsz],
                                       [1, 128]])
                        v.tensor_copy(dst, pv)._SEMINC(ev, 1)

            @b.tensor
            def _(t):
                for c in range(nchunks):
                    dv.wait(t, nchunks + 4 + c + 1)
                    for gq in range(NEVG):
                        gsz = min(4, NCD - 4 * gq)
                        if c > 0 or gq > 0:
                            ev.wait(t, NEVG * c + gq)
                        for kk in range(gsz):
                            dc = 4 * gq + kk
                            ins = t.transpose(
                                PS_TR[:, 128 * kk:128 * (kk + 1)],
                                xn_stB[c % 2].ap()[:, 128 * dc:128 * (dc + 1)],
                                ident[:])
                        ins._SEMINC(tr, 1)

    cq = wcast_sems("cwq")
    wldq = Sem(nc, "wldq")

    def _wq_prefetch(sy):
        cq["st"][0].wait(sy, 16 * ((NFS_H + 1) // 2))
        if NFS_H > 1:
            cq["st"][1].wait(sy, 16 * (NFS_H // 2))
        for dc in range(NCD):
            sy.dma_start_transpose(
                wT_h[:, WHALF * dc:WHALF * (dc + 1)],
                wb_d["wq"].ap()[0:WHALF, 128 * dc:128 * (dc + 1)]) \
                ._SEMINC(wldq, 16)

    emit_ln(x_d, 0, NT, xnT, S, "p1",
            aux_gp=lambda g: wcast_gp(g, "wq", cq),
            aux_sc=lambda s: wcast_scalar(s, "wq", cq),
            aux_sy=_wq_prefetch)
    dump("xnT", xnT[:])

    def emit_proj(wname, which, tag, cast_nm=None, pre_wld=None):
        bt = {"q": bqt, "k": bkt, "v": bvt}[which]
        spill = {"q": q_sc, "k": k_sc}.get(which)
        NB = TTW // 128
        GRP = NFS_H * NTT

        with nc.Block() as b:
            cast_q = wcast_sems(tag + "c") if cast_nm is not None else None
            wld = Sem(nc, f"{tag}w")
            mm = Sem(nc, f"{tag}m")
            evs = Sem(nc, f"{tag}e")
            sp = Sem(nc, f"{tag}s")
            trv = Sem(nc, f"{tag}t")
            evv = Sem(nc, f"{tag}v")

            @b.sync
            def _(sy):
                for half in range(2):
                    if half > 0:
                        mm.wait(sy, GRP)
                    if half == 0 and pre_wld is not None:
                        pass
                    else:
                        for dc in range(NCD):
                            sy.dma_start_transpose(
                                wT_h[:, WHALF * dc:WHALF * (dc + 1)],
                                wb_d[wname].ap()[WHALF * half:
                                                 WHALF * (half + 1),
                                                 128 * dc:128 * (dc + 1)]) \
                                ._SEMINC(wld, 16)
                    if spill is not None:
                        for fs in range(NFS_H):
                            for tt in range(NTT):
                                g = half * GRP + fs * NTT + tt
                                gfs = half * NFS_H + fs
                                evs.wait(sy, g + 1)
                                sy.dma_start(
                                    spill.ap()[128 * gfs:128 * (gfs + 1),
                                               TTW * tt:TTW * (tt + 1)],
                                    pkstg[g % 2][:, 0:TTW])._SEMINC(sp, 16)
                if spill is not None:
                    sp.wait(sy, sp.n)

            def _v_transpose(t, g):
                evs.wait(t, g + 1)
                if g > 0:
                    evv.wait(t, g)
                for kk in range(NB):
                    ins = t.transpose(PS_TR[:, 128 * kk:128 * (kk + 1)],
                                      vstg.ap()[:, 128 * kk:128 * (kk + 1)],
                                      ident[:])
                ins._SEMINC(trv, 1)

            @b.tensor
            def _(t):
                for half in range(2):
                    if half == 0 and pre_wld is not None:
                        pre_wld.wait(t, 16 * NCD)
                    else:
                        wld.wait(t, 16 * NCD * (half + 1) -
                                 (16 * NCD if pre_wld is not None else 0))
                    for fs in range(NFS_H):
                        for tt in range(NTT):
                            g = half * GRP + fs * NTT + tt
                            bank = g % 4
                            if g >= 4:
                                evs.wait(t, g - 3)
                            for dc in range(NCD):
                                ins = t.matmul(
                                    PS_A[:, 512 * bank:512 * bank + TTW],
                                    wT_h.ap()[:, WHALF * dc + 128 * fs:
                                              WHALF * dc + 128 * (fs + 1)],
                                    xnT.ap()[:, S * dc + TTW * tt:
                                             S * dc + TTW * (tt + 1)],
                                    start=(dc == 0), stop=(dc == NCD - 1))
                            ins._SEMINC(mm, 1)
                            if which == "v" and g >= 1:
                                _v_transpose(t, g - 1)
                if which == "v":
                    _v_transpose(t, 2 * GRP - 1)

            @b.gpsimd
            def _(g):
                if cast_nm is not None:
                    wcast_gp(g, cast_nm, cast_q)

            @b.scalar
            def _(s):
                for half in range(2):
                    for fs in range(NFS_H):
                        for tt in range(NTT):
                            g = half * GRP + fs * NTT + tt
                            bank = g % 4
                            gfs = half * NFS_H + fs
                            mm.wait(s, g + 1)
                            if which == "v":
                                if g > 0:
                                    trv.wait(s, g)
                                dst = vstg[:, 0:TTW]
                            else:
                                if g >= 2:
                                    sp.wait(s, 16 * (g - 1))
                                dst = pkstg[g % 2][:, 0:TTW]
                            s.activation(dst, PS_A[:, 512 * bank:512 * bank + TTW],
                                         AF.Identity, bias=bt.ap()[:, gfs:gfs + 1]) \
                                ._SEMINC(evs, 1)
                            if cast_nm is not None and g % 4 == 3 and g // 4 < NCD:
                                wcast_store_one(s, cast_nm, cast_q, g // 4)
                if cast_nm is not None:
                    for c in range(2 * GRP // 4, NCD):
                        wcast_store_one(s, cast_nm, cast_q, c)
                    cast_q["st"][0].wait(s, cast_q["st"][0].n)
                    cast_q["st"][1].wait(s, cast_q["st"][1].n)

            @b.vector
            def _(v):
                if which != "v":
                    v.engine_nop()
                    return
                for half in range(2):
                    for fs in range(NFS_H):
                        for tt in range(NTT):
                            g = half * GRP + fs * NTT + tt
                            gfs = half * NFS_H + fs
                            trv.wait(v, g + 1)
                            pv = bass.AP(PS_TR.tensor, PS_TR.offset,
                                         [[PS_TR.ap[0][0], 128], [128, NB],
                                          [64, 2], [1, 64]])
                            dst = bass.AP(VT.ap().tensor,
                                          VTC * (NB * tt) + 65 * (2 * gfs),
                                          [[VT.shape[1], 128], [VTC, NB],
                                           [65, 2], [1, 64]])
                            v.tensor_copy(dst, pv)._SEMINC(evv, 1)

    with nc.Block() as blk:
        @blk.vector
        def _(v):
            ap = bass.AP(VT.ap().tensor, 64,
                         [[VT.shape[1], 128], [VTC, NT], [65, NH], [1, 1]])
            v.memset(ap, 1.0)

    emit_proj("wq", "q", "p2q", cast_nm="wk", pre_wld=wldq)
    emit_proj("wk", "k", "p2k", cast_nm="wv")
    emit_proj("wv", "v", "p2v", cast_nm="wo")
    dump("VT", VT[:])

    for stg in range(NSA):
        nck = (stg + 1) * NTQA
        ndiag = NTQA
        nfull = nck - ndiag
        ngrp = (nfull + 3) // 4
        nexp_h = ngrp + ndiag
        with nc.Block() as blk:
            qsem = quant_sems("p3z%d" % stg)
            qlo, qhi = qsl[stg]
            _qsched = quant_sched(qlo, qhi)
            qld = [Sem(nc, "p3q%d_%d" % (stg, i)) for i in range(2)]
            kld = [Sem(nc, "p3e%d_%d" % (stg, i)) for i in range(2)]
            mm = Sem(nc, "p3m%d" % stg)
            ex = Sem(nc, "p3x%d" % stg)
            avc = Sem(nc, "p3a%d" % stg)
            rcp = Sem(nc, "p3r%d" % stg)
            oev = Sem(nc, "p3o%d" % stg)
            onr = Sem(nc, "p3n%d" % stg)
            osp = Sem(nc, "p3s%d" % stg)

            def expdone(h, c):
                if c < nfull:
                    return nexp_h * h + c // 4 + 1
                return nexp_h * h + ngrp + (c - nfull) + 1

            @blk.vector
            def _(v):
                for h in range(NH):
                    if h in _qsched:
                        quant_strip_vector(v, qlo + _qsched[h], _qsched[h], qsem)
                oev.wait(v, NH)
                v.reciprocal(rs_all[:, NTQA * NH:2 * NTQA * NH],
                             rs_all.ap()[:, 0:NTQA * NH])._SEMINC(rcp, 1)

            @blk.gpsimd
            def _(g):
                quant_gp(g, qlo, qhi, qsem)

            @blk.sync
            def _(sy):
                for h in range(NH):
                    if h >= 2:
                        mm.wait(sy, (h - 1) * nexp_h)
                    base = 64 * (h % 2)
                    sy.dma_start(qbuf.ap()[base:base + 64,
                                           TSA * (h % 2):TSA * (h % 2) + TSA],
                                 q_sc.ap()[64 * h:64 * (h + 1),
                                           TSA * stg:TSA * (stg + 1)]) \
                        ._SEMINC(qld[h % 2], 16)
                    sy.dma_start(kbuf.ap()[base:base + 64,
                                           2048 * (h % 2):2048 * (h % 2) + 128 * nck],
                                 k_sc.ap()[64 * h:64 * (h + 1), 0:128 * nck]) \
                        ._SEMINC(kld[h % 2], 16)
                onr.wait(sy, NH * NTQA)
                for k in range(NTQA):
                    sy.dma_start(o_sc.ap()[TSA * stg + 128 * k:
                                           TSA * stg + 128 * (k + 1), :],
                                 O_stg[:, D * k:D * (k + 1)])._SEMINC(osp, 16)
                osp.wait(sy, osp.n)

            @blk.tensor
            def _(t):
                for h in range(NH):
                    base = 64 * (h % 2)
                    qap = qbuf.ap()[base:base + 64,
                                    TSA * (h % 2):TSA * (h % 2) + TSA]
                    qld[h % 2].wait(t, 16 * (h // 2 + 1))
                    kld[h % 2].wait(t, 16 * (h // 2 + 1))
                    gi = 0
                    for g0 in range(0, nfull, 4):
                        gn = min(4, nfull - g0)
                        prev = nexp_h * h + gi - 1
                        if prev >= 0:
                            ex.wait(t, prev + 1)
                        for j in range(gn):
                            c = g0 + j
                            ins = t.matmul(
                                PS_A[:, 512 * j:512 * j + TSA],
                                kbuf.ap()[base:base + 64,
                                          2048 * (h % 2) + 128 * c:
                                          2048 * (h % 2) + 128 * (c + 1)],
                                qap, start=True, stop=True)
                        ins._SEMINC(mm, 1)
                        gi += 1
                    for j in range(ndiag):
                        c = nfull + j
                        prev = nexp_h * h + gi - 1
                        if prev >= 0:
                            ex.wait(t, prev + 1)
                        t.matmul(PS_A[:, 0:TSA],
                                 kbuf.ap()[base:base + 64,
                                           2048 * (h % 2) + 128 * c:
                                           2048 * (h % 2) + 128 * (c + 1)],
                                 qap, start=True, stop=False)
                        t.matmul(PS_A[:, 128 * j:128 * (j + 1)],
                                 ident[:], trineg[:],
                                 start=False, stop=True)._SEMINC(mm, 1)
                        gi += 1
                    if h >= 1:
                        oev.wait(t, h)
                    for c in range(nck):
                        ex.wait(t, expdone(h, c))
                        sub0 = max(0, c - nfull)
                        for sub in range(sub0, NTQA):
                            ins = t.matmul(
                                PS_C.ap()[:, 512 * sub:512 * sub + 65],
                                PT.ap()[:, TSA * c + 128 * sub:
                                        TSA * c + 128 * (sub + 1)],
                                VT.ap()[:, VTC * c + 65 * h:VTC * c + 65 * (h + 1)],
                                start=(c == 0),
                                stop=(c == nfull + sub))
                        ins._SEMINC(avc, 1)

            @blk.scalar
            def _(s):
                for h in range(NH):
                    gi = 0
                    for g0 in range(0, nfull, 4):
                        gn = min(4, nfull - g0)
                        mm.wait(s, nexp_h * h + gi + 1)
                        if h >= 1:
                            avc.wait(s, (h - 1) * nck + g0 + gn)
                        src = bass.AP(PS_A.ap().tensor, 0,
                                      [[2048, 128], [512, gn], [1, TSA]])
                        dsta = bass.AP(PT.ap().tensor, TSA * g0,
                                       [[PT.shape[1], 128], [TSA, gn], [1, TSA]])
                        s.activation(dsta, src, AF.Exp,
                                     bias=-2.0, scale=0.125)._SEMINC(ex, 1)
                        gi += 1
                    for j in range(ndiag):
                        c = nfull + j
                        mm.wait(s, nexp_h * h + gi + 1)
                        if h >= 1:
                            avc.wait(s, (h - 1) * nck + c + 1)
                        s.activation(PT[:, TSA * c + 128 * j:TSA * (c + 1)],
                                     PS_A[:, 128 * j:TSA], AF.Exp,
                                     bias=-2.0, scale=0.125)._SEMINC(ex, 1)
                        gi += 1
                    avc.wait(s, (h + 1) * nck)
                    for k in range(NTQA):
                        s.activation(
                            O_stg[:, D * k + 64 * h:D * k + 64 * (h + 1)],
                            PS_C.ap()[:, 512 * k:512 * k + 64], AF.Copy)
                    rsap = bass.AP(PS_C.ap().tensor, 64,
                                   [[2048, 128], [512, NTQA], [1, 1]])
                    s.activation(rs_all[:, NTQA * h:NTQA * (h + 1)], rsap,
                                 AF.Copy)._SEMINC(oev, 1)
                    if h in _qsched:
                        quant_strip_scalar(s, qlo + _qsched[h], _qsched[h], qsem)
                quant_flush_scalar(s, qlo, qhi, qsem)
                rcp.wait(s, 1)
                for h in range(NH):
                    for k in range(NTQA):
                        s.activation(
                            O_stg[:, D * k + 64 * h:D * k + 64 * (h + 1)],
                            O_stg[:, D * k + 64 * h:D * k + 64 * (h + 1)],
                            AF.Identity,
                            scale=rs_all.ap()[:, NTQA * (NH + h) + k:
                                              NTQA * (NH + h) + k + 1]) \
                            ._SEMINC(onr, 1)

    with nc.Block() as blk:
        otl = Sem(nc, "p3b_ot")
        wol = Sem(nc, "p3b_wo")
        xl = [Sem(nc, "p3b_xl%d" % i) for i in range(2)]
        mm = Sem(nc, "p3b_mm")
        ev = Sem(nc, "p3b_ev")
        sp = [Sem(nc, "p3b_sp%d" % i) for i in range(2)]

        @blk.sync
        def _(sy):
            for cc in range(NCD):
                sy.dma_start_transpose(OT[:, S * cc:S * (cc + 1)],
                                       o_sc.ap()[:, 128 * cc:128 * (cc + 1)]) \
                    ._SEMINC(otl, 16)
            for half in range(2):
                if half > 0:
                    mm.wait(sy, NT * NDT)
                for cc in range(NCD):
                    sy.dma_start_transpose(
                        woT_h[:, WHALF * cc:WHALF * (cc + 1)],
                        wb_d["wo"].ap()[WHALF * half:WHALF * (half + 1),
                                        128 * cc:128 * (cc + 1)]) \
                        ._SEMINC(wol, 16)
                for ts in range(NT):
                    bb = ts % 2
                    i = half * NT + ts
                    if i >= 2:
                        ev.wait(sy, (i - 1) * NDT)
                    sy.dma_start(xcb[bb][:],
                                 x_d.ap()[128 * ts:128 * (ts + 1),
                                          WHALF * half:WHALF * (half + 1)]) \
                        ._SEMINC(xl[bb], 16)

        @blk.tensor
        def _(t):
            otl.wait(t, 16 * NCD)
            for half in range(2):
                wol.wait(t, 16 * NCD * (half + 1))
                for ts in range(NT):
                    for dt in range(NDT):
                        g = (half * NT + ts) * NDT + dt
                        bank = g % 2
                        if g >= 2:
                            ev.wait(t, g - 1)
                        for cc in range(NCD):
                            t.matmul(PS_A[:, 512 * bank:512 * bank + DTW],
                                     OT.ap()[:, S * cc + 128 * ts:
                                             S * cc + 128 * (ts + 1)],
                                     woT_h.ap()[:, WHALF * cc + DTW * dt:
                                                WHALF * cc + DTW * (dt + 1)],
                                     start=(cc == 0), stop=False)
                        t.matmul(PS_A[:, 512 * bank:512 * bank + DTW],
                                 ones32.ap()[0:1, 0:128],
                                 bo_nat.ap()[0:1, WHALF * half + DTW * dt:
                                             WHALF * half + DTW * (dt + 1)],
                                 start=False, stop=True)._SEMINC(mm, 1)

        @blk.vector
        def _(v):
            for half in range(2):
                for ts in range(NT):
                    bb = ts % 2
                    i = half * NT + ts
                    xl[bb].wait(v, 16 * (i // 2 + 1))
                    if i >= 1:
                        sp[(i - 1) % 2].wait(v, 16 * ((i - 1) // 2 + 1))
                    for dt in range(NDT):
                        g = i * NDT + dt
                        bank = g % 2
                        mm.wait(v, g + 1)
                        v.tensor_add(r2stg[:, DTW * dt:DTW * (dt + 1)],
                                     PS_A[:, 512 * bank:512 * bank + DTW],
                                     xcb[bb][:, DTW * dt:DTW * (dt + 1)]) \
                            ._SEMINC(ev, 1)

        @blk.scalar
        def _(s):
            for half in range(2):
                for ts in range(NT):
                    i = half * NT + ts
                    ev.wait(s, (i + 1) * NDT)
                    s.dma_start(r2_sc.ap()[128 * ts:128 * (ts + 1),
                                           WHALF * half:WHALF * (half + 1)],
                                r2stg[:])._SEMINC(sp[i % 2], 16)
            sp[0].wait(s, sp[0].n)
            sp[1].wait(s, sp[1].n)

    for stg in range(NSF):
        gpre = Sem(nc, "p4p%d" % stg)

        def _gu_prefetch(sy):
            for dc in range(NCD):
                sy.dma_start_transpose(
                    gubuf[1][:, GUB * dc:GUB * (dc + 1)],
                    wgub_d.ap()[0:GUB, 128 * dc:128 * (dc + 1)]) \
                    ._SEMINC(gpre, 16)

        emit_ln(r2_sc, TSF * stg, NTSF, xn2T, TSF, "p4l%d" % stg,
                aux_sy=_gu_prefetch)

        ssbd = Sem(nc, "p4sb%d" % stg)
        with nc.Block() as blk:
            gl = [Sem(nc, "p4g%d_%d" % (stg, i)) for i in range(2)]
            mm = Sem(nc, "p4m%d" % stg)
            sil = Sem(nc, "p4s%d" % stg)
            hhs = Sem(nc, "p4h%d" % stg)
            sqs = Sem(nc, "p4q%d" % stg)
            ssm = Sem(nc, "p4z%d" % stg)

            @blk.sync
            def _(sy):
                for b in range(1, NGUB):
                    bb = (b + 1) % 2
                    if b >= 2:
                        mm.wait(sy, 2 * NPAIR * (b - 1))
                    for dc in range(NCD):
                        sy.dma_start_transpose(
                            gubuf[bb][:, GUB * dc:GUB * (dc + 1)],
                            wgub_d.ap()[GUB * b:GUB * (b + 1),
                                        128 * dc:128 * (dc + 1)]) \
                            ._SEMINC(gl[bb], 16)

            @blk.tensor
            def _(t):
                for b in range(NGUB):
                    bb = (b + 1) % 2
                    if b == 0:
                        gpre.wait(t, 16 * NCD)
                    else:
                        gl[bb].wait(t, 16 * NCD *
                                    (b // 2 + (1 if b % 2 else 0)))
                    for l in range(NPAIR):
                        hc = NPAIR * b + l
                        gb = 512 * (hc % 2)
                        ub = 1024 + 512 * (hc % 2)
                        if hc >= 2:
                            sil.wait(t, hc - 1)
                            hhs.wait(t, hc - 1)
                        for dc in range(NCD):
                            ins = t.matmul(PS_A[:, gb:gb + TSF],
                                           gubuf[bb].ap()[:, GUB * dc + 256 * l:
                                                          GUB * dc + 256 * l + 128],
                                           xn2T.ap()[:, TSF * dc:TSF * (dc + 1)],
                                           start=(dc == 0), stop=(dc == NCD - 1))
                        ins._SEMINC(mm, 1)
                        for dc in range(NCD):
                            ins = t.matmul(PS_A[:, ub:ub + TSF],
                                           gubuf[bb].ap()[:, GUB * dc + 256 * l + 128:
                                                          GUB * dc + 256 * (l + 1)],
                                           xn2T.ap()[:, TSF * dc:TSF * (dc + 1)],
                                           start=(dc == 0), stop=(dc == NCD - 1))
                        ins._SEMINC(mm, 1)
                        if hc >= 1:
                            sqs.wait(t, hc)
                            t.matmul(PS_C.ap()[0:1, 1024:1024 + TSF], onesbf.ap()[:, 0:1],
                                     sq_st.ap()[:, 0:TSF],
                                     start=(hc == 1), stop=False)._SEMINC(ssm, 1)
                sqs.wait(t, NHC)
                t.matmul(PS_C.ap()[0:1, 1024:1024 + TSF], onesbf.ap()[:, 0:1],
                         sq_st.ap()[:, 0:TSF], start=(NHC == 1), stop=True) \
                    ._SEMINC(ssm, 1)

            @blk.scalar
            def _(s):
                for hc in range(NHC):
                    mm.wait(s, 2 * hc + 1)
                    if hc >= 1:
                        hhs.wait(s, hc)
                    s.activation(sg_st[:, 0:TSF],
                                 PS_A[:, 512 * (hc % 2):512 * (hc % 2) + TSF],
                                 AF.Silu)._SEMINC(sil, 1)
                ssm.wait(s, NHC)
                s.activation(ssb[:], PS_C.ap()[0:1, 1024:1024 + TSF], AF.Copy) \
                    ._SEMINC(ssbd, 1)

            @blk.vector
            def _(v):
                for hc in range(NHC):
                    sil.wait(v, hc + 1)
                    mm.wait(v, 2 * hc + 2)
                    if hc >= 1:
                        ssm.wait(v, hc)
                    v.tensor_mul(hh[:, TSF * hc:TSF * (hc + 1)],
                                 sg_st.ap()[:, 0:TSF],
                                 PS_A[:, 1024 + 512 * (hc % 2):
                                      1024 + 512 * (hc % 2) + TSF]) \
                        ._SEMINC(hhs, 1)
                    v.drain()
                    v.tensor_mul(sq_st[:, 0:TSF], hh[:, TSF * hc:TSF * (hc + 1)],
                                 hh[:, TSF * hc:TSF * (hc + 1)])._SEMINC(sqs, 1)

        with nc.Block() as blk:
            tmm = Sem(nc, "p4t%d" % stg)
            tev = Sem(nc, "p4e%d" % stg)
            wl = [Sem(nc, "p4w%d_%d" % (stg, i)) for i in range(4)]
            rl = [Sem(nc, "p4r%d_%d" % (stg, i)) for i in range(2)]
            wcon = Sem(nc, "p4c%d" % stg)
            fmm = Sem(nc, "p4f%d" % stg)
            e1 = Sem(nc, "p4x%d" % stg)
            tr2 = Sem(nc, "p4y%d" % stg)
            e2 = Sem(nc, "p4u%d" % stg)
            acc = Sem(nc, "p4a%d" % stg, kind="sw")

            @blk.tensor
            def _(t):
                ssbd.wait(t, 1)
                for k in range(NTSF):
                    ins = t.matmul(PS_C.ap()[:, 512 + k:512 + k + 1],
                                   ssb.ap()[0:1, 128 * k:128 * (k + 1)],
                                   ones32.ap()[0:1, 0:1], start=True, stop=True)
                ins._SEMINC(tmm, 1)
                tev.wait(t, 3)
                for p in range(NP4):
                    if p > 0:
                        e2.wait(t, NTSF * DSOCT * p)
                    for hc in range(NHC):
                        bb = hc % 4
                        i = p * NHC + hc
                        wl[bb].wait(t, 16 * (i // 4 + 1))
                        for dsl in range(DSOCT):
                            ins = t.matmul(p4bank(dsl),
                                           wpbuf[bb].ap()[:, 128 * dsl:
                                                          128 * (dsl + 1)],
                                           hh.ap()[:, TSF * hc:TSF * (hc + 1)],
                                           start=(hc == 0), stop=(hc == NHC - 1))
                        ins._SEMINC(wcon, 1)
                    for dsl in range(DSOCT):
                        e1.wait(t, DSOCT * p + dsl + 1)
                        for kk in range(NTSF):
                            ins = t.transpose(
                                bass.AP(p4bank(dsl).tensor, p4bank(dsl).offset +
                                        128 * kk, [[2048, 128], [1, 128]]),
                                ev1_st.ap()[:, 128 * kk:128 * (kk + 1)], ident32[:])
                        ins._SEMINC(tr2, 1)

            @blk.sync
            def _(sy):
                for p in range(NP4):
                    for hc in range(NHC):
                        bb = hc % 4
                        i = p * NHC + hc
                        i = p * NHC + hc
                        bb = hc % 4
                        if i >= 4:
                            wcon.wait(sy, i - 3)
                        sy.dma_start_transpose(
                            wpbuf[bb][:, 0:DSOCT * 128],
                            wpb_d.ap()[DSOCT * 128 * p:DSOCT * 128 * (p + 1),
                                       128 * hc:128 * (hc + 1)]) \
                            ._SEMINC(wl[bb], 16)

            @blk.vector
            def _(v):
                tmm.wait(v, 1)
                v.tensor_copy(invt[:, 0:NTSF], PS_C.ap()[:, 512:512 + NTSF])
                v.drain()
                v.tensor_scalar(invt[:, 0:NTSF], invt[:, 0:NTSF],
                                float(1.0 / HGATE), 1.1920929e-07,
                                op0=AluOpType.mult, op1=AluOpType.add) \
                    ._SEMINC(tev, 1)
                tev.wait(v, 2)
                v.reciprocal(invt[:, 4:4 + NTSF], invt[:, 0:NTSF])._SEMINC(tev, 1)
                v.drain()
                for p in range(NP4):
                    wcon.wait(v, (p + 1) * NHC)
                    for dsl in range(DSOCT):
                        ds = DSOCT * p + dsl
                        if ds > 0:
                            tr2.wait(v, ds)
                        v.tensor_copy(ev1_st[:, 0:TSF], p4bank(dsl)) \
                            ._SEMINC(e1, 1)
                    tr2.wait(v, DSOCT * (p + 1))
                    for kk in range(NTSF):
                        rl[(p * NTSF + kk) % 2].wait(
                            v, 16 * ((p * NTSF + kk) // 2 + 1))
                        for dsl in range(DSOCT):
                            ds = DSOCT * p + dsl
                            pc = NTSF * DSOCT * p + kk * DSOCT + dsl
                            slot = pc % 8
                            if pc >= 8:
                                acc.wait(v, 16 * (pc - 7))
                            psT = bass.AP(p4bank(dsl).tensor,
                                          p4bank(dsl).offset + 128 * kk,
                                          [[2048, 128], [1, 128]])
                            v.scalar_tensor_tensor(
                                r3_st[:, 128 * slot:128 * (slot + 1)],
                                psT, invt.ap()[:, 4 + kk:5 + kk],
                                rjb[(p * NTSF + kk) % 2].ap()[:, 128 * dsl:
                                                              128 * (dsl + 1)],
                                op0=AluOpType.mult, op1=AluOpType.add) \
                                ._SEMINC(e2, 1)

            @blk.scalar
            def _(s):
                tev.wait(s, 1)
                s.activation(invt[:, 0:NTSF], invt[:, 0:NTSF], AF.Sqrt) \
                    ._SEMINC(tev, 1)
                for p in range(NP4):
                    for kk in range(NTSF):
                        i2 = p * NTSF + kk
                        bb2 = i2 % 2
                        if i2 >= 2:
                            j2 = i2 - 2
                            pj, kj = j2 // NTSF, j2 % NTSF
                            e2.wait(s, NTSF * DSOCT * pj + (kj + 1) * DSOCT)
                        s.dma_start(
                            rjb[bb2][:],
                            r2_sc.ap()[TSF * stg + 128 * kk:
                                       TSF * stg + 128 * (kk + 1),
                                       DSOCT * 128 * p:DSOCT * 128 * (p + 1)]) \
                            ._SEMINC(rl[bb2], 16)

            @blk.gpsimd
            def _(g):
                for p in range(NP4):
                    for kk in range(NTSF):
                        for dsl in range(DSOCT):
                            ds = DSOCT * p + dsl
                            pc = NTSF * DSOCT * p + kk * DSOCT + dsl
                            slot = pc % 8
                            e2.wait(g, pc + 1)
                            g.dma_start(
                                out_d.ap()[TSF * stg + 128 * kk:
                                           TSF * stg + 128 * (kk + 1),
                                           128 * ds:128 * (ds + 1)],
                                r3_st[:, 128 * slot:128 * (slot + 1)]) \
                                ._SEMINC(acc, 16)
                acc.wait(g, acc.n)

        if DBG and stg == 0:
            dump("hh", hh[:])
            dump("xn2T", xn2T[:])

    return nc


def kernel(**inputs):
    from concourse.bass_utils import run_bass_kernel_spmd
    cfg = dict(FULL_CFG)
    x = np.ascontiguousarray(inputs["L0_attn_hidden_states"])
    B = x.shape[0]
    nc = build(cfg)
    in_maps = []
    for b in range(B):
        m = {
            "x": x[b],
            "wq": inputs["L0_attn_wq"], "bq": inputs["L0_attn_bq"].reshape(1, -1),
            "wk": inputs["L0_attn_wk"], "bk": inputs["L0_attn_bk"].reshape(1, -1),
            "wv": inputs["L0_attn_wv"], "bv": inputs["L0_attn_bv"].reshape(1, -1),
            "wo": inputs["L0_attn_wo"], "bo": inputs["L0_attn_bo"].reshape(1, -1),
            "wgu": inputs["L0_ffn_w_gate_up"],
            "wp": inputs["L0_ffn_w_proj"],
        }
        in_maps.append(m)
    res = run_bass_kernel_spmd(nc, in_maps, core_ids=list(range(B)))
    return np.stack([np.asarray(r["out"]) for r in res.results], axis=0)


# version 38
# speedup vs baseline: 1.0170x; 1.0170x over previous
import numpy as np
import ml_dtypes
import concourse.bass as bass
import concourse.mybir as mybir
from concourse.alu_op_type import AluOpType
from concourse.bass_core_rust import BassInstruction as _BI


def _seminc(self, sem, k):
    sem.inc(self, k)
    return self


_BI._SEMINC = _seminc


F32 = mybir.dt.float32
BF16 = mybir.dt.bfloat16
AF = mybir.ActivationFunctionType
AX = mybir.AxisListType
bfdt = ml_dtypes.bfloat16

FULL_CFG = dict(S=2048, D=2048, NH=32, HGATE=8192, TSA=512, TSF=512, debug=False)
MINI_CFG = dict(S=256, D=256, NH=4, HGATE=1024, TSA=128, TSF=256, debug=True)


class SemPool:

    def __init__(self, nc, size=90, sw=14):
        self.handles = [nc.alloc_semaphore(f"s{i}") for i in range(size)]
        self.vals = [0] * size
        self.next = 0
        self.lim = size - sw
        self.swnext = self.lim

    def take(self, kind):
        if kind == "sw":
            i = self.swnext
            self.swnext = self.lim + (self.swnext + 1 - self.lim) % \
                (len(self.handles) - self.lim)
            return i
        i = self.next
        self.next = (self.next + 1) % self.lim
        return i


_POOL = [None]


class Sem:
    def __init__(self, nc, name, kind="hw"):
        pool = _POOL[0]
        self.pool = pool
        self.i = pool.take(kind)
        self.h = pool.handles[self.i]
        self.base = pool.vals[self.i]
        self.n = 0

    def inc(self, ins, k=1):
        ins.then_inc(self.h, k)
        self.n += k
        self.pool.vals[self.i] += k
        return self.n

    def wait(self, eng, v):
        eng.wait_ge(self.h, self.base + v)


def build(cfg):
    S, D, NH, HGATE = cfg["S"], cfg["D"], cfg["NH"], cfg["HGATE"]
    TSA, TSF = cfg["TSA"], cfg["TSF"]
    DBG = cfg.get("debug", False)
    HD = 64
    assert D == NH * HD
    NCD = D // 128
    NT = S // 128
    NSA = S // TSA
    NSF = S // TSF
    NTQA = TSA // 128
    NTSF = TSF // 128
    HGU2 = 2 * HGATE
    NHC = HGATE // 128
    WHALF = D // 2
    NFS_H = WHALF // 128
    TTW = min(512, S)
    NTT = S // TTW
    DTW = min(512, WHALF)
    NDT = WHALF // DTW
    GUB = min(1024, HGU2)
    NGUB = HGU2 // GUB
    NPAIR = GUB // 256
    VTC = 65 * NH
    NDS = D // 128
    DSOCT = min(8, NDS)
    NP4 = NDS // DSOCT
    assert TSF <= 512
    clamp_half = float(np.float32(bfdt(1e-8)) * np.float32(0.5))

    nc = bass.Bass("TRN2", target_bir_lowering=False, debug=False)
    _POOL[0] = SemPool(nc)

    x_d = nc.dram_tensor("x", [S, D], F32, kind="ExternalInput")
    w_d = {nm: nc.dram_tensor(nm, [D, D], F32, kind="ExternalInput")
           for nm in ("wq", "wk", "wv", "wo")}
    b_d = {nm: nc.dram_tensor(nm, [1, D], F32, kind="ExternalInput")
           for nm in ("bq", "bk", "bv", "bo")}
    wgu_d = nc.dram_tensor("wgu", [HGU2, D], F32, kind="ExternalInput")
    wp_d = nc.dram_tensor("wp", [D, HGATE], F32, kind="ExternalInput")
    out_d = nc.dram_tensor("out", [S, D], F32, kind="ExternalOutput")

    wb_d = {nm: nc.dram_tensor(nm + "b", [D, D], BF16)
            for nm in ("wq", "wk", "wv", "wo")}
    wgub_d = nc.dram_tensor("wgub", [HGU2, D], BF16)
    wpb_d = nc.dram_tensor("wpb", [D, HGATE], BF16)
    q_sc = nc.dram_tensor("q_sc", [D, S], BF16)
    k_sc = nc.dram_tensor("k_sc", [D, S], BF16)
    o_sc = nc.dram_tensor("o_sc", [S, D], BF16)
    r2_sc = nc.dram_tensor("r2_sc", [S, D], F32)

    SB_BASE = 18560
    cur = [SB_BASE]

    OFF = {}

    def sb(name, shape, dtype, at=None):
        if at is None:
            at = cur[0]
        OFF[name] = at
        t = nc.alloc_sbuf_tensor_at(name, shape, dtype, offset=at)
        esz = 2 if dtype == BF16 else 4
        end = at + int(np.prod(shape[1:])) * esz
        cur[0] = max(cur[0], (end + 63) // 64 * 64)
        return t

    consts = sb("consts", [128, 8], F32)
    iota_t = sb("iota_t", [128, 128], F32)
    ident = sb("ident", [128, 128], BF16)
    ident32 = sb("ident32", [128, 128], F32)
    tri = sb("tri", [128, 128], BF16)
    trineg = sb("trineg", [128, 128], BF16)
    ones32 = sb("ones32", [128, 128], F32)
    onesbf = sb("onesbf", [128, 16], BF16)
    bqt = sb("bqt", [128, NCD], F32)
    bkt = sb("bkt", [128, NCD], F32)
    bvt = sb("bvt", [128, NCD], F32)
    stats = sb("stats", [128, 128], F32)
    rs_t = sb("rs_t", [128, 8], F32)
    rs_all = sb("rs_all", [128, 2 * 4 * 32], F32)
    invt = sb("invt", [128, 8], F32)
    gsumb = [sb(f"gsumb{i}", [128, 128], F32) for i in range(2)]
    shalfb = [sb(f"shalfb{i}", [128, 128], BF16) for i in range(2)]
    bo_nat = sb("bo_nat", [1, D], F32)
    ssb = sb("ssb", [1, TSF], F32)
    qbuf = sb("qbuf", [128, 2 * TSA], BF16)
    sg_st = sb("sg_st", [128, 512], F32)
    sq_st = sb("sq_st", [128, 512], BF16)
    ev1_st = sb("ev1_st", [128, 512], F32)
    r3_st = sb("r3_st", [128, 1024], F32)
    xn_st = sb("xn_st", [128, D], BF16)
    vstg = sb("vstg", [128, 512], BF16)
    pkstg = [sb(f"pkstg{i}", [128, 512], BF16) for i in range(2)]
    kbuf = sb("kbuf", [128, 2 * 2048], BF16, OFF["ev1_st"])

    R1 = (cur[0] + 63) // 64 * 64
    R1_SZ = max(2 * NCD * S, 36 * D)
    R2 = R1 + R1_SZ
    R2_SZ = max(2 * NT * VTC, 4 * NCD * GUB,
                2 * NCD * WHALF + 8 * WHALF, 12 * D)
    R3 = R2 + R2_SZ
    R3_SZ = max(2 * NCD * WHALF,
                2 * NT * TSA + 2 * NTQA * D,
                4 * WHALF,
                2 * NCD * TSF + 16 * DSOCT * 128)
    assert R3 + R3_SZ <= 229376, (R1, R2, R3, R3 + R3_SZ)

    xnT = sb("xnT", [128, NCD * S], BF16, R1)
    OT = sb("OT", [128, NCD * S], BF16, R1)
    hh = sb("hh", [128, NHC * TSF], BF16, R1)
    wstrip = [sb(f"wstrip{i}", [128, 4 * D], BF16, R1 + 8 * D * i) for i in range(2)]
    qA = sb("qA", [128, 4 * D], BF16, R1 + 16 * D)
    qB = sb("qB", [128, 4 * D], BF16, R1 + 24 * D)
    cstb = [sb(f"cstb{i}", [128, D], BF16, R1 + 32 * D + 2 * D * i) for i in range(2)]

    VT = sb("VT", [128, NT * VTC], BF16, R2)
    woT_h = sb("woT_h", [128, NCD * WHALF], BF16, R2)
    xcb = [sb(f"xcb{i}", [128, WHALF], F32, R2 + 2 * NCD * WHALF + 4 * WHALF * i)
           for i in range(2)]
    gubuf = [sb(f"gubuf{i}", [128, NCD * GUB], BF16, R2 + 2 * NCD * GUB * i)
             for i in range(2)]
    xc1 = [sb(f"xc1_{i}", [128, D], F32, R2 + 4 * D * i) for i in range(2)]
    sqsc = sb("sqsc", [128, D], F32, R2 + 8 * D)
    sqscB = sb("sqscB", [128, D], F32, R2 + 12 * D)

    wT_h = sb("wT_h", [128, NCD * WHALF], BF16, R3)
    PT = sb("PT", [128, NT * TSA], BF16, R3)
    O_stg = sb("O_stg", [128, NTQA * D], BF16, R3 + 2 * NT * TSA)
    r2stg = sb("r2stg", [128, WHALF], F32, R3)
    xn2T = sb("xn2T", [128, NCD * TSF], BF16, R3)
    wpbuf = [sb(f"wpbuf{i}", [128, DSOCT * 128], BF16,
                R3 + 2 * NCD * TSF + 2 * DSOCT * 128 * i) for i in range(4)]
    xn_stB = [sb(f"xn_stB{i}", [128, D], BF16,
                 R3 + 2 * NCD * TSF + 2 * D * i) for i in range(2)]
    rjb = [sb(f"rjb{i}", [128, DSOCT * 128], F32,
              R3 + 2 * NCD * TSF + 8 * DSOCT * 128 + 4 * DSOCT * 128 * i)
           for i in range(2)]

    PS_A = nc.alloc_psum_tensor("PS_A", [128, 2048], F32)
    PS_C = nc.alloc_psum_tensor("PS_C", [128, 2048], F32)
    PS_TR = PS_C.ap().bitcast(BF16)[:, 3072:4096]

    def p4bank(dsl):
        if dsl < 4:
            return PS_A.ap()[:, 512 * dsl:512 * dsl + TSF]
        return PS_C.ap()[:, 512 * (dsl - 4):512 * (dsl - 4) + TSF]

    nc.const_aps.register(F32, 0.125, consts.ap()[:, 0:1])
    nc.const_aps.register(F32, -2.0, consts.ap()[:, 1:2])
    nc.const_aps.register(F32, 1e-5, consts.ap()[:, 2:3])
    nc.const_aps.register(F32, 1.1920929e-07, consts.ap()[:, 3:4])
    nc.const_aps.register(F32, 1.0, consts.ap()[:, 5:6])

    dbg_d = {}
    if DBG:
        for nm, sh, dt in [("xnT", [128, NCD * S], BF16),
                           ("VT", [128, NT * VTC], BF16),
                           ("hh", [128, NHC * TSF], BF16),
                           ("xn2T", [128, NCD * TSF], BF16),
                           ("Ostg", [128, NTQA * D], BF16)]:
            dbg_d[nm] = nc.dram_tensor("dbg_" + nm, sh, dt)

    def dump(name, src_ap):
        if name not in dbg_d:
            return
        with nc.Block() as b:
            sem = Sem(nc, f"dbg{name}")

            @b.sync
            def _(sy):
                sy.dma_start(dbg_d[name].ap(), src_ap)._SEMINC(sem, 16)
                sem.wait(sy, 16)

    with nc.Block() as blk:
        @blk.vector
        def _(v):
            for i, val in enumerate([0.125, -2.0, 1e-5, 1.1920929e-07, 0.0, 1.0]):
                v.memset(consts[:, i:i + 1], val)
            v.memset(ones32[:], 1.0)
            v.memset(onesbf[:], 1.0)

        @blk.gpsimd
        def _(g):
            g.iota(iota_t[:], [[1, 128]], channel_multiplier=-1,
                   allow_small_or_imprecise_dtypes=True)

    with nc.Block() as blk:
        @blk.vector
        def _(v):
            v.tensor_scalar(ident[:], iota_t[:], 0.0, None, op0=AluOpType.is_equal)
            v.tensor_scalar(ident32[:], iota_t[:], 0.0, None, op0=AluOpType.is_equal)
            v.tensor_scalar(tri[:], iota_t[:], 0.0, None, op0=AluOpType.is_ge)
            v.tensor_scalar(trineg[:], iota_t[:], 0.0, -1e9,
                            op0=AluOpType.is_lt, op1=AluOpType.mult)

    with nc.Block() as blk:
        ld = Sem(nc, "p0a_ld")
        mm = Sem(nc, "p0a_mm")
        ev = Sem(nc, "p0a_ev")
        dests = [("bq", bqt), ("bk", bkt), ("bv", bvt), ("bo", None)]

        @blk.sync
        def _(sy):
            for i, (nm, dst) in enumerate(dests):
                if i > 0:
                    ev.wait(sy, i)
                sy.dma_start(bo_nat[:], b_d[nm].ap())._SEMINC(ld, 16)

        @blk.tensor
        def _(t):
            for i, (nm, dst) in enumerate(dests):
                ld.wait(t, 16 * (i + 1))
                if dst is None:
                    t.nop()._SEMINC(mm, 1)
                    continue
                for j in range(NCD):
                    ins = t.matmul(PS_C.ap()[:, 512 + j:512 + j + 1],
                                   bo_nat.ap()[0:1, 128 * j:128 * (j + 1)],
                                   ones32.ap()[0:1, 0:1], start=True, stop=True)
                ins._SEMINC(mm, 1)

        @blk.vector
        def _(v):
            for i, (nm, dst) in enumerate(dests):
                mm.wait(v, i + 1)
                if dst is None:
                    v.engine_nop()._SEMINC(ev, 1)
                else:
                    v.tensor_copy(dst[:, 0:NCD], PS_C.ap()[:, 512:512 + NCD])._SEMINC(ev, 1)

    def wcast_sems(tag):
        return dict(ld=[Sem(nc, f"{tag}l{i}", kind="sw") for i in range(2)],
                    st=[Sem(nc, f"{tag}s{i}") for i in range(2)])

    def wcast_gp(g, nm, q):
        for c in range(NCD):
            b = c % 2
            if c >= 2:
                q["st"][b].wait(g, 16 * (c // 2))
            g.dma_start(cstb[b][:], w_d[nm].ap()[128 * c:128 * (c + 1), :]) \
                ._SEMINC(q["ld"][b], 16)

    def wcast_store_one(s, nm, q, c):
        b = c % 2
        q["ld"][b].wait(s, 16 * (c // 2 + 1))
        s.dma_start(wb_d[nm].ap()[128 * c:128 * (c + 1), :], cstb[b][:]) \
            ._SEMINC(q["st"][b], 16)

    def wcast_scalar(s, nm, q):
        for c in range(NCD):
            wcast_store_one(s, nm, q, c)
        q["st"][0].wait(s, q["st"][0].n)
        q["st"][1].wait(s, q["st"][1].n)

    strips = [("gu", si) for si in range(HGU2 // 512)] + \
             [("p", si) for si in range(D // 128)]

    def sfree(kind):
        return 4 * D if kind == "gu" else HGATE

    def shbcast(b, nf):
        return shalfb[b].ap()[:, 0:nf // 64].unsqueeze(2) \
            .broadcast_to([128, nf // 64, 64])

    st_counts = []
    _n = 0
    for kind, si in strips:
        _n += 4 if kind == "gu" else 1
        st_counts.append(_n)

    def _p0c_store(s, strip, st):
        kind, si = strip
        nf = sfree(kind)
        if kind == "gu":
            for c in range(4):
                gsub = si * 4 + c
                if 512 * si + 128 * c < HGATE:
                    row = 256 * gsub
                else:
                    row = 256 * (gsub - HGATE // 128) + 128
                s.dma_start(wgub_d.ap()[row:row + 128, :],
                            qA[:, D * c:D * (c + 1)])._SEMINC(st, 16)
        else:
            s.dma_start(wpb_d.ap()[128 * si:128 * (si + 1), :],
                        qA[:, 0:nf])._SEMINC(st, 16)

    def quant_sems(tag):
        return dict(ld=[Sem(nc, f"{tag}l{i}", kind="sw") for i in range(2)],
                    sml=Sem(nc, f"{tag}m"),
                    ads=Sem(nc, f"{tag}a"), st=Sem(nc, f"{tag}s"))

    def quant_gp(g, lo, hi, q):
        ld, ads = q["ld"], q["ads"]
        for i in range(lo, hi):
            kind, si = strips[i]
            j = i - lo
            b = j % 2
            nf = sfree(kind)
            if j >= 2:
                ads.wait(g, j - 1)
            if kind == "gu":
                src = wgu_d.ap()[512 * si:512 * (si + 1), :].rearrange(
                    "(c p) d -> p c d", p=128)
                dst = wstrip[b].ap()[:, 0:nf].rearrange("p (c d) -> p c d", d=D)
            else:
                src = wp_d.ap()[128 * si:128 * (si + 1), :]
                dst = wstrip[b][:, 0:nf]
            g.dma_start(dst, src)._SEMINC(ld[b], 16)

    def quant_strip_scalar(s, i, j, q):
        ads, st = q["ads"], q["st"]
        if j > 0:
            ads.wait(s, j)
            _p0c_store(s, strips[i - 1], st)

    def quant_flush_scalar(s, lo, hi, q):
        if hi > lo:
            q["ads"].wait(s, hi - lo)
            _p0c_store(s, strips[hi - 1], q["st"])
            q["st"].wait(s, q["st"].n)

    def quant_strip_vector(v, i, j, q):
        kind, si = strips[i]
        b = j % 2
        nf = sfree(kind)
        ld, sml, ads, st = q["ld"], q["sml"], q["ads"], q["st"]
        ld[b].wait(v, 16 * (j // 2 + 1))
        wgv0 = wstrip[b].ap()[:, 0:nf].rearrange("p (g e) -> p g e", e=64)
        v.tensor_reduce(gsumb[b].ap()[:, 0:nf // 64], wgv0, axis=AX.X,
                        op=AluOpType.add, apply_absolute_value=True)
        v.drain()
        v.tensor_scalar(shalfb[b][:, 0:nf // 64], gsumb[b][:, 0:nf // 64],
                        float(1.0 / 128.0), clamp_half,
                        op0=AluOpType.mult, op1=AluOpType.max)._SEMINC(sml, 1)
        if j >= 1:
            st.wait(v, 16 * st_local(i - j, j))
        v.tensor_scalar(qA.ap()[:, 0:nf].bitcast(mybir.dt.uint16),
                        wstrip[b].ap()[:, 0:nf].bitcast(mybir.dt.uint16),
                        0x7FFF, None, op0=AluOpType.bitwise_and)
        v.drain()
        v.tensor_tensor(qA.ap()[:, 0:nf].rearrange("p (g e) -> p g e", e=64),
                        qA.ap()[:, 0:nf].rearrange("p (g e) -> p g e", e=64),
                        shbcast(b, nf), op=AluOpType.is_gt)
        v.drain()
        v.scalar_tensor_tensor(
            qA.ap()[:, 0:nf].rearrange("p (g e) -> p g e", e=64),
            qA.ap()[:, 0:nf].rearrange("p (g e) -> p g e", e=64), 2.0,
            shbcast(b, nf),
            op0=AluOpType.mult, op1=AluOpType.mult)
        v.tensor_scalar(qB.ap()[:, 0:nf].bitcast(mybir.dt.uint16),
                        wstrip[b].ap()[:, 0:nf].bitcast(mybir.dt.uint16),
                        0x8000, None, op0=AluOpType.bitwise_and)
        v.drain()
        v.tensor_tensor(qA.ap()[:, 0:nf].bitcast(mybir.dt.uint16),
                        qA.ap()[:, 0:nf].bitcast(mybir.dt.uint16),
                        qB.ap()[:, 0:nf].bitcast(mybir.dt.uint16),
                        op=AluOpType.bitwise_or)._SEMINC(ads, 1)

    NSTR = len(strips)
    qsl = []
    per = (NSTR + NSA - 1) // NSA
    for stg in range(NSA):
        qsl.append((min(stg * per, NSTR), min((stg + 1) * per, NSTR)))

    def st_local(lo, j):
        n = 0
        for t in range(lo, lo + j):
            n += 4 if strips[t][0] == "gu" else 1
        return n

    def quant_sched(lo, hi):
        n = hi - lo
        return {int((j + 1) * NH / (n + 1)): j for j in range(n)} if n else {}

    def emit_ln(src_dram, row0, nchunks, dest_T, dest_w, tag,
                aux_gp=None, aux_sc=None):
        NEVG = (NCD + 3) // 4

        with nc.Block() as b:
            ld = [Sem(nc, f"{tag}l{i}") for i in range(2)]
            sa = Sem(nc, f"{tag}s")
            dv = Sem(nc, f"{tag}d")
            tr = Sem(nc, f"{tag}t")
            ev = Sem(nc, f"{tag}e")

            @b.sync
            def _(sy):
                for i in range(2 * nchunks):
                    bb = i % 2
                    if i >= 2:
                        j = i - 2
                        if j < nchunks:
                            sa.wait(sy, j + 1)
                            dv.wait(sy, j + 1)
                        else:
                            dv.wait(sy, nchunks + 4 + (j - nchunks) + 1)
                    c = i % nchunks
                    sy.dma_start(xc1[bb][:],
                                 src_dram.ap()[row0 + 128 * c:row0 + 128 * (c + 1), :]) \
                        ._SEMINC(ld[bb], 16)

            @b.gpsimd
            def _(g):
                if aux_gp is not None:
                    aux_gp(g)

            @b.scalar
            def _(s):
                for c in range(nchunks):
                    bb = c % 2
                    ld[bb].wait(s, 16 * (c // 2 + 1))
                    if c >= 2:
                        sa.wait(s, c - 1)
                    s.activation((sqsc if c % 2 == 0 else sqscB)[:],
                                 xc1[bb][:], AF.Square,
                                 accum_out=stats[:, 16 + c:17 + c])._SEMINC(sa, 1)
                if aux_sc is not None:
                    aux_sc(s)
                dv.wait(s, nchunks + 3)
                s.activation(stats[:, 80:80 + nchunks], stats[:, 64:64 + nchunks],
                             AF.Sqrt, bias=1e-5)._SEMINC(sa, 1)

            @b.vector
            def _(v):
                for c in range(nchunks):
                    bb = c % 2
                    ld[bb].wait(v, 16 * (c // 2 + 1))
                    v.reduce_sum(stats[:, c:c + 1], xc1[bb][:], axis=AX.X) \
                        ._SEMINC(dv, 1)
                sa.wait(v, nchunks)
                v.drain()
                v.tensor_scalar(stats[:, 32:32 + nchunks], stats[:, 0:nchunks],
                                float(1.0 / D), None, op0=AluOpType.mult) \
                    ._SEMINC(dv, 1)
                v.drain()
                v.tensor_mul(stats[:, 48:48 + nchunks], stats[:, 32:32 + nchunks],
                             stats[:, 32:32 + nchunks])._SEMINC(dv, 1)
                v.drain()
                v.scalar_tensor_tensor(stats[:, 64:64 + nchunks],
                                       stats[:, 16:16 + nchunks], float(1.0 / D),
                                       stats[:, 48:48 + nchunks],
                                       op0=AluOpType.mult, op1=AluOpType.subtract) \
                    ._SEMINC(dv, 1)
                sa.wait(v, nchunks + 1)
                v.reciprocal(stats[:, 96:96 + nchunks], stats[:, 80:80 + nchunks]) \
                    ._SEMINC(dv, 1)
                v.drain()
                def ln_apply(c):
                    i = nchunks + c
                    bb = i % 2
                    ld[bb].wait(v, 16 * (i // 2 + 1))
                    if c >= 2:
                        tr.wait(v, NEVG * (c - 1))
                    v.tensor_scalar(xn_stB[c % 2][:], xc1[bb][:],
                                    stats.ap()[:, 32 + c:33 + c],
                                    stats.ap()[:, 96 + c:97 + c],
                                    op0=AluOpType.subtract, op1=AluOpType.mult) \
                        ._SEMINC(dv, 1)

                ln_apply(0)
                for c in range(nchunks):
                    if c + 1 < nchunks:
                        ln_apply(c + 1)
                    for gq in range(NEVG):
                        gsz = min(4, NCD - 4 * gq)
                        tr.wait(v, NEVG * c + gq + 1)
                        pv = bass.AP(PS_TR.tensor, PS_TR.offset,
                                     [[PS_TR.ap[0][0], 128], [128, gsz], [1, 128]])
                        dst = bass.AP(dest_T.ap().tensor,
                                      dest_w * (4 * gq) + 128 * c,
                                      [[dest_T.shape[1], 128], [dest_w, gsz],
                                       [1, 128]])
                        v.tensor_copy(dst, pv)._SEMINC(ev, 1)

            @b.tensor
            def _(t):
                for c in range(nchunks):
                    dv.wait(t, nchunks + 4 + c + 1)
                    for gq in range(NEVG):
                        gsz = min(4, NCD - 4 * gq)
                        if c > 0 or gq > 0:
                            ev.wait(t, NEVG * c + gq)
                        for kk in range(gsz):
                            dc = 4 * gq + kk
                            ins = t.transpose(
                                PS_TR[:, 128 * kk:128 * (kk + 1)],
                                xn_stB[c % 2].ap()[:, 128 * dc:128 * (dc + 1)],
                                ident[:])
                        ins._SEMINC(tr, 1)

    def emit_ln(src_dram, row0, nchunks, dest_T, dest_w, tag,
                aux_gp=None, aux_sc=None):
        NEVG = (NCD + 3) // 4

        with nc.Block() as b:
            ld = [Sem(nc, f"{tag}l{i}") for i in range(2)]
            sa = Sem(nc, f"{tag}s")
            dv = Sem(nc, f"{tag}d")
            tr = Sem(nc, f"{tag}t")
            ev = Sem(nc, f"{tag}e")

            @b.sync
            def _(sy):
                for i in range(2 * nchunks):
                    bb = i % 2
                    if i >= 2:
                        j = i - 2
                        if j < nchunks:
                            sa.wait(sy, j + 1)
                            dv.wait(sy, j + 1)
                        else:
                            dv.wait(sy, nchunks + 4 + (j - nchunks) + 1)
                    c = i % nchunks
                    sy.dma_start(xc1[bb][:],
                                 src_dram.ap()[row0 + 128 * c:row0 + 128 * (c + 1), :]) \
                        ._SEMINC(ld[bb], 16)

            @b.gpsimd
            def _(g):
                if aux_gp is not None:
                    aux_gp(g)

            @b.scalar
            def _(s):
                for c in range(nchunks):
                    bb = c % 2
                    ld[bb].wait(s, 16 * (c // 2 + 1))
                    if c >= 2:
                        sa.wait(s, c - 1)
                    s.activation((sqsc if c % 2 == 0 else sqscB)[:],
                                 xc1[bb][:], AF.Square,
                                 accum_out=stats[:, 16 + c:17 + c])._SEMINC(sa, 1)
                if aux_sc is not None:
                    aux_sc(s)
                dv.wait(s, nchunks + 3)
                s.activation(stats[:, 80:80 + nchunks], stats[:, 64:64 + nchunks],
                             AF.Sqrt, bias=1e-5)._SEMINC(sa, 1)

            @b.vector
            def _(v):
                for c in range(nchunks):
                    bb = c % 2
                    ld[bb].wait(v, 16 * (c // 2 + 1))
                    v.reduce_sum(stats[:, c:c + 1], xc1[bb][:], axis=AX.X) \
                        ._SEMINC(dv, 1)
                sa.wait(v, nchunks)
                v.drain()
                v.tensor_scalar(stats[:, 32:32 + nchunks], stats[:, 0:nchunks],
                                float(1.0 / D), None, op0=AluOpType.mult) \
                    ._SEMINC(dv, 1)
                v.drain()
                v.tensor_mul(stats[:, 48:48 + nchunks], stats[:, 32:32 + nchunks],
                             stats[:, 32:32 + nchunks])._SEMINC(dv, 1)
                v.drain()
                v.scalar_tensor_tensor(stats[:, 64:64 + nchunks],
                                       stats[:, 16:16 + nchunks], float(1.0 / D),
                                       stats[:, 48:48 + nchunks],
                                       op0=AluOpType.mult, op1=AluOpType.subtract) \
                    ._SEMINC(dv, 1)
                sa.wait(v, nchunks + 1)
                v.reciprocal(stats[:, 96:96 + nchunks], stats[:, 80:80 + nchunks]) \
                    ._SEMINC(dv, 1)
                v.drain()
                def ln_apply(c):
                    i = nchunks + c
                    bb = i % 2
                    ld[bb].wait(v, 16 * (i // 2 + 1))
                    if c >= 2:
                        tr.wait(v, NEVG * (c - 1))
                    v.tensor_scalar(xn_stB[c % 2][:], xc1[bb][:],
                                    stats.ap()[:, 32 + c:33 + c],
                                    stats.ap()[:, 96 + c:97 + c],
                                    op0=AluOpType.subtract, op1=AluOpType.mult) \
                        ._SEMINC(dv, 1)

                ln_apply(0)
                for c in range(nchunks):
                    if c + 1 < nchunks:
                        ln_apply(c + 1)
                    for gq in range(NEVG):
                        gsz = min(4, NCD - 4 * gq)
                        tr.wait(v, NEVG * c + gq + 1)
                        pv = bass.AP(PS_TR.tensor, PS_TR.offset,
                                     [[PS_TR.ap[0][0], 128], [128, gsz], [1, 128]])
                        dst = bass.AP(dest_T.ap().tensor,
                                      dest_w * (4 * gq) + 128 * c,
                                      [[dest_T.shape[1], 128], [dest_w, gsz],
                                       [1, 128]])
                        v.tensor_copy(dst, pv)._SEMINC(ev, 1)

            @b.tensor
            def _(t):
                for c in range(nchunks):
                    dv.wait(t, nchunks + 4 + c + 1)
                    for gq in range(NEVG):
                        gsz = min(4, NCD - 4 * gq)
                        if c > 0 or gq > 0:
                            ev.wait(t, NEVG * c + gq)
                        for kk in range(gsz):
                            dc = 4 * gq + kk
                            ins = t.transpose(
                                PS_TR[:, 128 * kk:128 * (kk + 1)],
                                xn_stB[c % 2].ap()[:, 128 * dc:128 * (dc + 1)],
                                ident[:])
                        ins._SEMINC(tr, 1)

    cq = wcast_sems("cwq")
    emit_ln(x_d, 0, NT, xnT, S, "p1",
            aux_gp=lambda g: wcast_gp(g, "wq", cq),
            aux_sc=lambda s: wcast_scalar(s, "wq", cq))
    dump("xnT", xnT[:])

    def emit_proj(wname, which, tag, cast_nm=None):
        bt = {"q": bqt, "k": bkt, "v": bvt}[which]
        spill = {"q": q_sc, "k": k_sc}.get(which)
        NB = TTW // 128
        GRP = NFS_H * NTT

        with nc.Block() as b:
            cast_q = wcast_sems(tag + "c") if cast_nm is not None else None
            wld = Sem(nc, f"{tag}w")
            mm = Sem(nc, f"{tag}m")
            evs = Sem(nc, f"{tag}e")
            sp = Sem(nc, f"{tag}s")
            trv = Sem(nc, f"{tag}t")
            evv = Sem(nc, f"{tag}v")

            @b.sync
            def _(sy):
                for half in range(2):
                    if half > 0:
                        mm.wait(sy, GRP)
                    for dc in range(NCD):
                        sy.dma_start_transpose(
                            wT_h[:, WHALF * dc:WHALF * (dc + 1)],
                            wb_d[wname].ap()[WHALF * half:WHALF * (half + 1),
                                             128 * dc:128 * (dc + 1)]) \
                            ._SEMINC(wld, 16)
                    if spill is not None:
                        for fs in range(NFS_H):
                            for tt in range(NTT):
                                g = half * GRP + fs * NTT + tt
                                gfs = half * NFS_H + fs
                                evs.wait(sy, g + 1)
                                sy.dma_start(
                                    spill.ap()[128 * gfs:128 * (gfs + 1),
                                               TTW * tt:TTW * (tt + 1)],
                                    pkstg[g % 2][:, 0:TTW])._SEMINC(sp, 16)
                if spill is not None:
                    sp.wait(sy, sp.n)

            def _v_transpose(t, g):
                evs.wait(t, g + 1)
                if g > 0:
                    evv.wait(t, g)
                for kk in range(NB):
                    ins = t.transpose(PS_TR[:, 128 * kk:128 * (kk + 1)],
                                      vstg.ap()[:, 128 * kk:128 * (kk + 1)],
                                      ident[:])
                ins._SEMINC(trv, 1)

            @b.tensor
            def _(t):
                for half in range(2):
                    wld.wait(t, 16 * NCD * (half + 1))
                    for fs in range(NFS_H):
                        for tt in range(NTT):
                            g = half * GRP + fs * NTT + tt
                            bank = g % 4
                            if g >= 4:
                                evs.wait(t, g - 3)
                            for dc in range(NCD):
                                ins = t.matmul(
                                    PS_A[:, 512 * bank:512 * bank + TTW],
                                    wT_h.ap()[:, WHALF * dc + 128 * fs:
                                              WHALF * dc + 128 * (fs + 1)],
                                    xnT.ap()[:, S * dc + TTW * tt:
                                             S * dc + TTW * (tt + 1)],
                                    start=(dc == 0), stop=(dc == NCD - 1))
                            ins._SEMINC(mm, 1)
                            if which == "v" and g >= 1:
                                _v_transpose(t, g - 1)
                if which == "v":
                    _v_transpose(t, 2 * GRP - 1)

            @b.gpsimd
            def _(g):
                if cast_nm is not None:
                    wcast_gp(g, cast_nm, cast_q)

            @b.scalar
            def _(s):
                for half in range(2):
                    for fs in range(NFS_H):
                        for tt in range(NTT):
                            g = half * GRP + fs * NTT + tt
                            bank = g % 4
                            gfs = half * NFS_H + fs
                            mm.wait(s, g + 1)
                            if which == "v":
                                if g > 0:
                                    trv.wait(s, g)
                                dst = vstg[:, 0:TTW]
                            else:
                                if g >= 2:
                                    sp.wait(s, 16 * (g - 1))
                                dst = pkstg[g % 2][:, 0:TTW]
                            s.activation(dst, PS_A[:, 512 * bank:512 * bank + TTW],
                                         AF.Identity, bias=bt.ap()[:, gfs:gfs + 1]) \
                                ._SEMINC(evs, 1)
                            if cast_nm is not None and g % 4 == 3 and g // 4 < NCD:
                                wcast_store_one(s, cast_nm, cast_q, g // 4)
                if cast_nm is not None:
                    for c in range(2 * GRP // 4, NCD):
                        wcast_store_one(s, cast_nm, cast_q, c)
                    cast_q["st"][0].wait(s, cast_q["st"][0].n)
                    cast_q["st"][1].wait(s, cast_q["st"][1].n)

            @b.vector
            def _(v):
                if which != "v":
                    v.engine_nop()
                    return
                for half in range(2):
                    for fs in range(NFS_H):
                        for tt in range(NTT):
                            g = half * GRP + fs * NTT + tt
                            gfs = half * NFS_H + fs
                            trv.wait(v, g + 1)
                            pv = bass.AP(PS_TR.tensor, PS_TR.offset,
                                         [[PS_TR.ap[0][0], 128], [128, NB],
                                          [64, 2], [1, 64]])
                            dst = bass.AP(VT.ap().tensor,
                                          VTC * (NB * tt) + 65 * (2 * gfs),
                                          [[VT.shape[1], 128], [VTC, NB],
                                           [65, 2], [1, 64]])
                            v.tensor_copy(dst, pv)._SEMINC(evv, 1)

    with nc.Block() as blk:
        @blk.vector
        def _(v):
            ap = bass.AP(VT.ap().tensor, 64,
                         [[VT.shape[1], 128], [VTC, NT], [65, NH], [1, 1]])
            v.memset(ap, 1.0)

    emit_proj("wq", "q", "p2q", cast_nm="wk")
    emit_proj("wk", "k", "p2k", cast_nm="wv")
    emit_proj("wv", "v", "p2v", cast_nm="wo")
    dump("VT", VT[:])

    for stg in range(NSA):
        nck = (stg + 1) * NTQA
        ndiag = NTQA
        nfull = nck - ndiag
        ngrp = (nfull + 3) // 4
        nexp_h = ngrp + ndiag
        with nc.Block() as blk:
            qsem = quant_sems("p3z%d" % stg)
            qlo, qhi = qsl[stg]
            _qsched = quant_sched(qlo, qhi)
            qld = [Sem(nc, "p3q%d_%d" % (stg, i)) for i in range(2)]
            kld = [Sem(nc, "p3e%d_%d" % (stg, i)) for i in range(2)]
            mm = Sem(nc, "p3m%d" % stg)
            ex = Sem(nc, "p3x%d" % stg)
            avc = Sem(nc, "p3a%d" % stg)
            rcp = Sem(nc, "p3r%d" % stg)
            oev = Sem(nc, "p3o%d" % stg)
            onr = Sem(nc, "p3n%d" % stg)
            osp = Sem(nc, "p3s%d" % stg)

            def expdone(h, c):
                if c < nfull:
                    return nexp_h * h + c // 4 + 1
                return nexp_h * h + ngrp + (c - nfull) + 1

            @blk.vector
            def _(v):
                for h in range(NH):
                    if h in _qsched:
                        quant_strip_vector(v, qlo + _qsched[h], _qsched[h], qsem)
                oev.wait(v, NH)
                v.reciprocal(rs_all[:, NTQA * NH:2 * NTQA * NH],
                             rs_all.ap()[:, 0:NTQA * NH])._SEMINC(rcp, 1)

            @blk.gpsimd
            def _(g):
                quant_gp(g, qlo, qhi, qsem)

            @blk.sync
            def _(sy):
                for h in range(NH):
                    if h >= 2:
                        mm.wait(sy, (h - 1) * nexp_h)
                    base = 64 * (h % 2)
                    sy.dma_start(qbuf.ap()[base:base + 64,
                                           TSA * (h % 2):TSA * (h % 2) + TSA],
                                 q_sc.ap()[64 * h:64 * (h + 1),
                                           TSA * stg:TSA * (stg + 1)]) \
                        ._SEMINC(qld[h % 2], 16)
                    sy.dma_start(kbuf.ap()[base:base + 64,
                                           2048 * (h % 2):2048 * (h % 2) + 128 * nck],
                                 k_sc.ap()[64 * h:64 * (h + 1), 0:128 * nck]) \
                        ._SEMINC(kld[h % 2], 16)
                onr.wait(sy, NH * NTQA)
                for k in range(NTQA):
                    sy.dma_start(o_sc.ap()[TSA * stg + 128 * k:
                                           TSA * stg + 128 * (k + 1), :],
                                 O_stg[:, D * k:D * (k + 1)])._SEMINC(osp, 16)
                osp.wait(sy, osp.n)

            @blk.tensor
            def _(t):
                for h in range(NH):
                    base = 64 * (h % 2)
                    qap = qbuf.ap()[base:base + 64,
                                    TSA * (h % 2):TSA * (h % 2) + TSA]
                    qld[h % 2].wait(t, 16 * (h // 2 + 1))
                    kld[h % 2].wait(t, 16 * (h // 2 + 1))
                    gi = 0
                    for g0 in range(0, nfull, 4):
                        gn = min(4, nfull - g0)
                        prev = nexp_h * h + gi - 1
                        if prev >= 0:
                            ex.wait(t, prev + 1)
                        for j in range(gn):
                            c = g0 + j
                            ins = t.matmul(
                                PS_A[:, 512 * j:512 * j + TSA],
                                kbuf.ap()[base:base + 64,
                                          2048 * (h % 2) + 128 * c:
                                          2048 * (h % 2) + 128 * (c + 1)],
                                qap, start=True, stop=True)
                        ins._SEMINC(mm, 1)
                        gi += 1
                    for j in range(ndiag):
                        c = nfull + j
                        prev = nexp_h * h + gi - 1
                        if prev >= 0:
                            ex.wait(t, prev + 1)
                        t.matmul(PS_A[:, 0:TSA],
                                 kbuf.ap()[base:base + 64,
                                           2048 * (h % 2) + 128 * c:
                                           2048 * (h % 2) + 128 * (c + 1)],
                                 qap, start=True, stop=False)
                        t.matmul(PS_A[:, 128 * j:128 * (j + 1)],
                                 ident[:], trineg[:],
                                 start=False, stop=True)._SEMINC(mm, 1)
                        gi += 1
                    if h >= 1:
                        oev.wait(t, h)
                    for c in range(nck):
                        ex.wait(t, expdone(h, c))
                        sub0 = max(0, c - nfull)
                        for sub in range(sub0, NTQA):
                            ins = t.matmul(
                                PS_C.ap()[:, 512 * sub:512 * sub + 65],
                                PT.ap()[:, TSA * c + 128 * sub:
                                        TSA * c + 128 * (sub + 1)],
                                VT.ap()[:, VTC * c + 65 * h:VTC * c + 65 * (h + 1)],
                                start=(c == 0),
                                stop=(c == nfull + sub))
                        ins._SEMINC(avc, 1)

            @blk.scalar
            def _(s):
                for h in range(NH):
                    gi = 0
                    for g0 in range(0, nfull, 4):
                        gn = min(4, nfull - g0)
                        mm.wait(s, nexp_h * h + gi + 1)
                        if h >= 1:
                            avc.wait(s, (h - 1) * nck + g0 + gn)
                        src = bass.AP(PS_A.ap().tensor, 0,
                                      [[2048, 128], [512, gn], [1, TSA]])
                        dsta = bass.AP(PT.ap().tensor, TSA * g0,
                                       [[PT.shape[1], 128], [TSA, gn], [1, TSA]])
                        s.activation(dsta, src, AF.Exp,
                                     bias=-2.0, scale=0.125)._SEMINC(ex, 1)
                        gi += 1
                    for j in range(ndiag):
                        c = nfull + j
                        mm.wait(s, nexp_h * h + gi + 1)
                        if h >= 1:
                            avc.wait(s, (h - 1) * nck + c + 1)
                        s.activation(PT[:, TSA * c + 128 * j:TSA * (c + 1)],
                                     PS_A[:, 128 * j:TSA], AF.Exp,
                                     bias=-2.0, scale=0.125)._SEMINC(ex, 1)
                        gi += 1
                    avc.wait(s, (h + 1) * nck)
                    for k in range(NTQA):
                        s.activation(
                            O_stg[:, D * k + 64 * h:D * k + 64 * (h + 1)],
                            PS_C.ap()[:, 512 * k:512 * k + 64], AF.Copy)
                    rsap = bass.AP(PS_C.ap().tensor, 64,
                                   [[2048, 128], [512, NTQA], [1, 1]])
                    s.activation(rs_all[:, NTQA * h:NTQA * (h + 1)], rsap,
                                 AF.Copy)._SEMINC(oev, 1)
                    if h in _qsched:
                        quant_strip_scalar(s, qlo + _qsched[h], _qsched[h], qsem)
                quant_flush_scalar(s, qlo, qhi, qsem)
                rcp.wait(s, 1)
                for h in range(NH):
                    for k in range(NTQA):
                        s.activation(
                            O_stg[:, D * k + 64 * h:D * k + 64 * (h + 1)],
                            O_stg[:, D * k + 64 * h:D * k + 64 * (h + 1)],
                            AF.Identity,
                            scale=rs_all.ap()[:, NTQA * (NH + h) + k:
                                              NTQA * (NH + h) + k + 1]) \
                            ._SEMINC(onr, 1)

    with nc.Block() as blk:
        otl = Sem(nc, "p3b_ot")
        wol = Sem(nc, "p3b_wo")
        xl = [Sem(nc, "p3b_xl%d" % i) for i in range(2)]
        mm = Sem(nc, "p3b_mm")
        ev = Sem(nc, "p3b_ev")
        sp = [Sem(nc, "p3b_sp%d" % i) for i in range(2)]

        @blk.sync
        def _(sy):
            for cc in range(NCD):
                sy.dma_start_transpose(OT[:, S * cc:S * (cc + 1)],
                                       o_sc.ap()[:, 128 * cc:128 * (cc + 1)]) \
                    ._SEMINC(otl, 16)
            for half in range(2):
                if half > 0:
                    mm.wait(sy, NT * NDT)
                for cc in range(NCD):
                    sy.dma_start_transpose(
                        woT_h[:, WHALF * cc:WHALF * (cc + 1)],
                        wb_d["wo"].ap()[WHALF * half:WHALF * (half + 1),
                                        128 * cc:128 * (cc + 1)]) \
                        ._SEMINC(wol, 16)
                for ts in range(NT):
                    bb = ts % 2
                    i = half * NT + ts
                    if i >= 2:
                        ev.wait(sy, (i - 1) * NDT)
                    sy.dma_start(xcb[bb][:],
                                 x_d.ap()[128 * ts:128 * (ts + 1),
                                          WHALF * half:WHALF * (half + 1)]) \
                        ._SEMINC(xl[bb], 16)

        @blk.tensor
        def _(t):
            otl.wait(t, 16 * NCD)
            for half in range(2):
                wol.wait(t, 16 * NCD * (half + 1))
                for ts in range(NT):
                    for dt in range(NDT):
                        g = (half * NT + ts) * NDT + dt
                        bank = g % 2
                        if g >= 2:
                            ev.wait(t, g - 1)
                        for cc in range(NCD):
                            t.matmul(PS_A[:, 512 * bank:512 * bank + DTW],
                                     OT.ap()[:, S * cc + 128 * ts:
                                             S * cc + 128 * (ts + 1)],
                                     woT_h.ap()[:, WHALF * cc + DTW * dt:
                                                WHALF * cc + DTW * (dt + 1)],
                                     start=(cc == 0), stop=False)
                        t.matmul(PS_A[:, 512 * bank:512 * bank + DTW],
                                 ones32.ap()[0:1, 0:128],
                                 bo_nat.ap()[0:1, WHALF * half + DTW * dt:
                                             WHALF * half + DTW * (dt + 1)],
                                 start=False, stop=True)._SEMINC(mm, 1)

        @blk.vector
        def _(v):
            for half in range(2):
                for ts in range(NT):
                    bb = ts % 2
                    i = half * NT + ts
                    xl[bb].wait(v, 16 * (i // 2 + 1))
                    if i >= 1:
                        sp[(i - 1) % 2].wait(v, 16 * ((i - 1) // 2 + 1))
                    for dt in range(NDT):
                        g = i * NDT + dt
                        bank = g % 2
                        mm.wait(v, g + 1)
                        v.tensor_add(r2stg[:, DTW * dt:DTW * (dt + 1)],
                                     PS_A[:, 512 * bank:512 * bank + DTW],
                                     xcb[bb][:, DTW * dt:DTW * (dt + 1)]) \
                            ._SEMINC(ev, 1)

        @blk.scalar
        def _(s):
            for half in range(2):
                for ts in range(NT):
                    i = half * NT + ts
                    ev.wait(s, (i + 1) * NDT)
                    s.dma_start(r2_sc.ap()[128 * ts:128 * (ts + 1),
                                           WHALF * half:WHALF * (half + 1)],
                                r2stg[:])._SEMINC(sp[i % 2], 16)
            sp[0].wait(s, sp[0].n)
            sp[1].wait(s, sp[1].n)

    for stg in range(NSF):
        emit_ln(r2_sc, TSF * stg, NTSF, xn2T, TSF, "p4l%d" % stg)

        ssbd = Sem(nc, "p4sb%d" % stg)
        with nc.Block() as blk:
            gl = [Sem(nc, "p4g%d_%d" % (stg, i)) for i in range(2)]
            mm = Sem(nc, "p4m%d" % stg)
            sil = Sem(nc, "p4s%d" % stg)
            hhs = Sem(nc, "p4h%d" % stg)
            sqs = Sem(nc, "p4q%d" % stg)
            ssm = Sem(nc, "p4z%d" % stg)

            @blk.sync
            def _(sy):
                for b in range(NGUB):
                    bb = b % 2
                    if b >= 2:
                        mm.wait(sy, 2 * NPAIR * (b - 1))
                    for dc in range(NCD):
                        sy.dma_start_transpose(
                            gubuf[bb][:, GUB * dc:GUB * (dc + 1)],
                            wgub_d.ap()[GUB * b:GUB * (b + 1),
                                        128 * dc:128 * (dc + 1)]) \
                            ._SEMINC(gl[bb], 16)

            @blk.tensor
            def _(t):
                for b in range(NGUB):
                    bb = b % 2
                    gl[bb].wait(t, 16 * NCD * (b // 2 + 1))
                    for l in range(NPAIR):
                        hc = NPAIR * b + l
                        gb = 512 * (hc % 2)
                        ub = 1024 + 512 * (hc % 2)
                        if hc >= 2:
                            sil.wait(t, hc - 1)
                            hhs.wait(t, hc - 1)
                        for dc in range(NCD):
                            ins = t.matmul(PS_A[:, gb:gb + TSF],
                                           gubuf[bb].ap()[:, GUB * dc + 256 * l:
                                                          GUB * dc + 256 * l + 128],
                                           xn2T.ap()[:, TSF * dc:TSF * (dc + 1)],
                                           start=(dc == 0), stop=(dc == NCD - 1))
                        ins._SEMINC(mm, 1)
                        for dc in range(NCD):
                            ins = t.matmul(PS_A[:, ub:ub + TSF],
                                           gubuf[bb].ap()[:, GUB * dc + 256 * l + 128:
                                                          GUB * dc + 256 * (l + 1)],
                                           xn2T.ap()[:, TSF * dc:TSF * (dc + 1)],
                                           start=(dc == 0), stop=(dc == NCD - 1))
                        ins._SEMINC(mm, 1)
                        if hc >= 1:
                            sqs.wait(t, hc)
                            t.matmul(PS_C.ap()[0:1, 1024:1024 + TSF], onesbf.ap()[:, 0:1],
                                     sq_st.ap()[:, 0:TSF],
                                     start=(hc == 1), stop=False)._SEMINC(ssm, 1)
                sqs.wait(t, NHC)
                t.matmul(PS_C.ap()[0:1, 1024:1024 + TSF], onesbf.ap()[:, 0:1],
                         sq_st.ap()[:, 0:TSF], start=(NHC == 1), stop=True) \
                    ._SEMINC(ssm, 1)

            @blk.scalar
            def _(s):
                for hc in range(NHC):
                    mm.wait(s, 2 * hc + 1)
                    if hc >= 1:
                        hhs.wait(s, hc)
                    s.activation(sg_st[:, 0:TSF],
                                 PS_A[:, 512 * (hc % 2):512 * (hc % 2) + TSF],
                                 AF.Silu)._SEMINC(sil, 1)
                ssm.wait(s, NHC)
                s.activation(ssb[:], PS_C.ap()[0:1, 1024:1024 + TSF], AF.Copy) \
                    ._SEMINC(ssbd, 1)

            @blk.vector
            def _(v):
                for hc in range(NHC):
                    sil.wait(v, hc + 1)
                    mm.wait(v, 2 * hc + 2)
                    if hc >= 1:
                        ssm.wait(v, hc)
                    v.tensor_mul(hh[:, TSF * hc:TSF * (hc + 1)],
                                 sg_st.ap()[:, 0:TSF],
                                 PS_A[:, 1024 + 512 * (hc % 2):
                                      1024 + 512 * (hc % 2) + TSF]) \
                        ._SEMINC(hhs, 1)
                    v.drain()
                    v.tensor_mul(sq_st[:, 0:TSF], hh[:, TSF * hc:TSF * (hc + 1)],
                                 hh[:, TSF * hc:TSF * (hc + 1)])._SEMINC(sqs, 1)

        with nc.Block() as blk:
            tmm = Sem(nc, "p4t%d" % stg)
            tev = Sem(nc, "p4e%d" % stg)
            wl = [Sem(nc, "p4w%d_%d" % (stg, i)) for i in range(4)]
            rl = [Sem(nc, "p4r%d_%d" % (stg, i)) for i in range(2)]
            wcon = Sem(nc, "p4c%d" % stg)
            fmm = Sem(nc, "p4f%d" % stg)
            e1 = Sem(nc, "p4x%d" % stg)
            tr2 = Sem(nc, "p4y%d" % stg)
            e2 = Sem(nc, "p4u%d" % stg)
            acc = Sem(nc, "p4a%d" % stg, kind="sw")

            @blk.tensor
            def _(t):
                ssbd.wait(t, 1)
                for k in range(NTSF):
                    ins = t.matmul(PS_C.ap()[:, 512 + k:512 + k + 1],
                                   ssb.ap()[0:1, 128 * k:128 * (k + 1)],
                                   ones32.ap()[0:1, 0:1], start=True, stop=True)
                ins._SEMINC(tmm, 1)
                tev.wait(t, 3)
                for p in range(NP4):
                    if p > 0:
                        e2.wait(t, NTSF * DSOCT * p)
                    for hc in range(NHC):
                        bb = hc % 4
                        i = p * NHC + hc
                        wl[bb].wait(t, 16 * (i // 4 + 1))
                        for dsl in range(DSOCT):
                            ins = t.matmul(p4bank(dsl),
                                           wpbuf[bb].ap()[:, 128 * dsl:
                                                          128 * (dsl + 1)],
                                           hh.ap()[:, TSF * hc:TSF * (hc + 1)],
                                           start=(hc == 0), stop=(hc == NHC - 1))
                        ins._SEMINC(wcon, 1)
                    for dsl in range(DSOCT):
                        e1.wait(t, DSOCT * p + dsl + 1)
                        for kk in range(NTSF):
                            ins = t.transpose(
                                bass.AP(p4bank(dsl).tensor, p4bank(dsl).offset +
                                        128 * kk, [[2048, 128], [1, 128]]),
                                ev1_st.ap()[:, 128 * kk:128 * (kk + 1)], ident32[:])
                        ins._SEMINC(tr2, 1)

            @blk.sync
            def _(sy):
                for p in range(NP4):
                    for hc in range(NHC):
                        bb = hc % 4
                        i = p * NHC + hc
                        i = p * NHC + hc
                        bb = hc % 4
                        if i >= 4:
                            wcon.wait(sy, i - 3)
                        sy.dma_start_transpose(
                            wpbuf[bb][:, 0:DSOCT * 128],
                            wpb_d.ap()[DSOCT * 128 * p:DSOCT * 128 * (p + 1),
                                       128 * hc:128 * (hc + 1)]) \
                            ._SEMINC(wl[bb], 16)

            @blk.vector
            def _(v):
                tmm.wait(v, 1)
                v.tensor_copy(invt[:, 0:NTSF], PS_C.ap()[:, 512:512 + NTSF])
                v.drain()
                v.tensor_scalar(invt[:, 0:NTSF], invt[:, 0:NTSF],
                                float(1.0 / HGATE), 1.1920929e-07,
                                op0=AluOpType.mult, op1=AluOpType.add) \
                    ._SEMINC(tev, 1)
                tev.wait(v, 2)
                v.reciprocal(invt[:, 4:4 + NTSF], invt[:, 0:NTSF])._SEMINC(tev, 1)
                v.drain()
                for p in range(NP4):
                    wcon.wait(v, (p + 1) * NHC)
                    for dsl in range(DSOCT):
                        ds = DSOCT * p + dsl
                        if ds > 0:
                            tr2.wait(v, ds)
                        v.tensor_copy(ev1_st[:, 0:TSF], p4bank(dsl)) \
                            ._SEMINC(e1, 1)
                    tr2.wait(v, DSOCT * (p + 1))
                    for kk in range(NTSF):
                        rl[(p * NTSF + kk) % 2].wait(
                            v, 16 * ((p * NTSF + kk) // 2 + 1))
                        for dsl in range(DSOCT):
                            ds = DSOCT * p + dsl
                            pc = NTSF * DSOCT * p + kk * DSOCT + dsl
                            slot = pc % 8
                            if pc >= 8:
                                acc.wait(v, 16 * (pc - 7))
                            psT = bass.AP(p4bank(dsl).tensor,
                                          p4bank(dsl).offset + 128 * kk,
                                          [[2048, 128], [1, 128]])
                            v.scalar_tensor_tensor(
                                r3_st[:, 128 * slot:128 * (slot + 1)],
                                psT, invt.ap()[:, 4 + kk:5 + kk],
                                rjb[(p * NTSF + kk) % 2].ap()[:, 128 * dsl:
                                                              128 * (dsl + 1)],
                                op0=AluOpType.mult, op1=AluOpType.add) \
                                ._SEMINC(e2, 1)

            @blk.scalar
            def _(s):
                tev.wait(s, 1)
                s.activation(invt[:, 0:NTSF], invt[:, 0:NTSF], AF.Sqrt) \
                    ._SEMINC(tev, 1)
                for p in range(NP4):
                    for kk in range(NTSF):
                        i2 = p * NTSF + kk
                        bb2 = i2 % 2
                        if i2 >= 2:
                            j2 = i2 - 2
                            pj, kj = j2 // NTSF, j2 % NTSF
                            e2.wait(s, NTSF * DSOCT * pj + (kj + 1) * DSOCT)
                        s.dma_start(
                            rjb[bb2][:],
                            r2_sc.ap()[TSF * stg + 128 * kk:
                                       TSF * stg + 128 * (kk + 1),
                                       DSOCT * 128 * p:DSOCT * 128 * (p + 1)]) \
                            ._SEMINC(rl[bb2], 16)

            @blk.gpsimd
            def _(g):
                for p in range(NP4):
                    for kk in range(NTSF):
                        for dsl in range(DSOCT):
                            ds = DSOCT * p + dsl
                            pc = NTSF * DSOCT * p + kk * DSOCT + dsl
                            slot = pc % 8
                            e2.wait(g, pc + 1)
                            g.dma_start(
                                out_d.ap()[TSF * stg + 128 * kk:
                                           TSF * stg + 128 * (kk + 1),
                                           128 * ds:128 * (ds + 1)],
                                r3_st[:, 128 * slot:128 * (slot + 1)]) \
                                ._SEMINC(acc, 16)
                acc.wait(g, acc.n)

        if DBG and stg == 0:
            dump("hh", hh[:])
            dump("xn2T", xn2T[:])

    return nc


def kernel(**inputs):
    from concourse.bass_utils import run_bass_kernel_spmd
    cfg = dict(FULL_CFG)
    x = np.ascontiguousarray(inputs["L0_attn_hidden_states"])
    B = x.shape[0]
    nc = build(cfg)
    in_maps = []
    for b in range(B):
        m = {
            "x": x[b],
            "wq": inputs["L0_attn_wq"], "bq": inputs["L0_attn_bq"].reshape(1, -1),
            "wk": inputs["L0_attn_wk"], "bk": inputs["L0_attn_bk"].reshape(1, -1),
            "wv": inputs["L0_attn_wv"], "bv": inputs["L0_attn_bv"].reshape(1, -1),
            "wo": inputs["L0_attn_wo"], "bo": inputs["L0_attn_bo"].reshape(1, -1),
            "wgu": inputs["L0_ffn_w_gate_up"],
            "wp": inputs["L0_ffn_w_proj"],
        }
        in_maps.append(m)
    res = run_bass_kernel_spmd(nc, in_maps, core_ids=list(range(B)))
    return np.stack([np.asarray(r["out"]) for r in res.results], axis=0)
